# version 03 (final)
# baseline (speedup 1.0000x reference)
from contextlib import ExitStack
import numpy as np
import ml_dtypes
import concourse.bass as bass
import concourse.mybir as mybir
from concourse.bass_utils import run_bass_kernel_spmd

F32 = mybir.dt.float32
BF16 = mybir.dt.bfloat16
AF = mybir.ActivationFunctionType
ALU = mybir.AluOpType
AX = mybir.AxisListType
BF = ml_dtypes.bfloat16

NCORES = 8
D = 1024
TOKS = 16384
TPC = TOKS // NCORES
NT = TPC // 128
SEQ = 4096
MAGIC = 12582912.0

ENGINES = ("sync", "gpsimd", "tensor", "vector", "scalar")


class Prog:

    def __init__(self, nc):
        self.nc = nc
        self.steps = {e: [] for e in ENGINES}
        self.count = {e: 0 for e in ENGINES}
        self._sems = {}
        self._ctxs = []

    NPOOL = 16

    def __enter__(self):
        for e in ENGINES:
            cm = self.nc.semaphore("prog_" + e)
            self._sems[e] = cm.__enter__()
            self._ctxs.append(cm)
        for i in range(self.NPOOL):
            cm = self.nc.semaphore(f"dmax{i}")
            self._sems[f"x{i}"] = cm.__enter__()
            self.count[f"x{i}"] = 0
            self._ctxs.append(cm)
        self._xi = 0
        self._xj = 0
        cm = self.nc.sbuf_tensor("prog_dummy", [1, 8], mybir.dt.float32)
        self._dummy = cm.__enter__()
        self._ctxs.append(cm)
        return self

    def xkey(self, cls=0):
        if cls == 0:
            k = f"x{self._xi % (self.NPOOL // 2)}"
            self._xi += 1
        else:
            k = f"x{self.NPOOL // 2 + self._xj % (self.NPOOL // 2)}"
            self._xj += 1
        return k

    def __exit__(self, *a):
        for cm in reversed(self._ctxs):
            cm.__exit__(*a)
        return False

    def sem(self, eng):
        return self._sems[eng]

    def step(self, eng, fn, deps=(), inc=1, self_inc=False, key=None):
        deps = [d for d in deps if d is not None]
        key = key or eng
        after = self.count[key] + inc
        self.steps[eng].append((fn, tuple(deps), inc, self_inc, key))
        self.count[key] = after
        return (key, after)

    def last(self, eng):
        return (eng, self.count[eng])

    def final_wait(self, eng):
        tgt = self.count[eng]
        self.step(eng, lambda e: e.wait_ge(self._sems[eng], tgt), inc=0, self_inc=True)

    def emit(self):
        with self.nc.Block() as blk:
            for eng in ENGINES:
                if not self.steps[eng]:
                    continue

                def body(e, eng=eng):
                    satisfied = {}
                    for fn, deps, inc, self_inc, key in self.steps[eng]:
                        for (deng, dcount) in deps:
                            if deng == eng:
                                continue
                            if satisfied.get(deng, 0) >= dcount:
                                continue
                            e.wait_ge(self._sems[deng], dcount)
                            satisfied[deng] = dcount
                        out = fn(e)
                        if not self_inc:
                            assert out is not None, f"step on {eng} returned no instruction"
                            if eng == "vector" or eng == "gpsimd":
                                e.drain()
                                e.memset(self._dummy[0:1, 0:2], 0.0).then_inc(self._sems[eng], inc)
                            elif eng == "scalar":
                                e.drain()
                                e.copy(self._dummy[0:1, 2:4], self._dummy[0:1, 4:6]).then_inc(self._sems[eng], inc)
                            else:
                                out.then_inc(self._sems[eng], inc)

                getattr(blk, eng)(body)


def dma_step(p, eng, pairs, deps=(), exact=True):
    key = p.xkey() if exact else eng
    sem = p.sem(key)

    def fn(e):
        for dst, src in pairs:
            e.dma_start(out=dst, in_=src).then_inc(sem, 16)
    return p.step(eng, fn, deps=deps, inc=16 * len(pairs), self_inc=True, key=key)



def split_tokens(x):
    xf = np.ascontiguousarray(np.asarray(x).reshape(TOKS, D))
    return [xf[c * TPC:(c + 1) * TPC] for c in range(NCORES)]


def join_tokens(parts):
    return np.ascontiguousarray(np.concatenate(parts, 0)).reshape(4, SEQ, D).astype(np.float32)


IDENT128 = np.eye(128, dtype=BF)



def emit_ln(p, xf_t, xn_t, st, sq, t, dep_load, g_ap=None, b_ap=None, xn_f32=None, war_dep=None, vscr=None):
    a = st[:, 4 * t + 0:4 * t + 1]
    b = st[:, 4 * t + 1:4 * t + 2]
    c = st[:, 4 * t + 2:4 * t + 3]
    d = st[:, 4 * t + 3:4 * t + 4]

    def scl(s):
        s.activation(sq[:, :], xf_t, AF.Square, accum_out=b)
        return s.activation(sq[:, :], xf_t, AF.Copy, accum_out=a)
    ev_s = p.step("scalar", scl, deps=[dep_load])

    def vec1(v):
        v.tensor_scalar(a, a, 1.0 / D, None, ALU.mult)
        v.drain()
        v.tensor_scalar(b, b, 1.0 / D, None, ALU.mult)
        v.drain()
        v.tensor_scalar(c, a, a, None, ALU.mult)
        v.drain()
        v.tensor_tensor(c, b, c, ALU.subtract)
        v.drain()
        return v.tensor_scalar(c, c, 1e-5, None, ALU.add)
    ev_v1 = p.step("vector", vec1, deps=[ev_s])

    ev_s2 = p.step("scalar", lambda s: s.activation(c, c, AF.Sqrt), deps=[ev_v1])

    def vec2(v):
        v.reciprocal(c, c)
        v.drain()
        v.tensor_tensor(d, a, c, ALU.mult)
        v.drain()
        v.tensor_scalar(d, d, -1.0, None, ALU.mult)
        v.drain()
        outs = []
        if g_ap is None:
            if xn_f32 is not None:
                v.tensor_scalar(xn_f32, xf_t, c, d, ALU.mult, ALU.add)
            last = v.tensor_scalar(xn_t, xf_t, c, d, ALU.mult, ALU.add)
        else:
            vs = vscr if vscr is not None else sq[:, :]
            v.tensor_scalar(vs, xf_t, c, d, ALU.mult, ALU.add)
            v.tensor_tensor(vs, vs, g_ap, ALU.mult)
            if xn_f32 is not None:
                v.tensor_tensor(xn_f32, vs, b_ap, ALU.add)
            last = v.tensor_tensor(xn_t, vs, b_ap, ALU.add)
        return last
    return p.step("vector", vec2, deps=[ev_s2, war_dep])


class TransposeHelper:

    def __init__(self, p, nc, psT, ident_b):
        self.p, self.nc = p, nc
        self.prev_evac = None

    def transpose_tile(self, src_bf_t, dst_3d, tcol, dep_src, ngroups=8, evac_deps=()):
        p = self.p
        key = p.xkey(1)
        sem = p.sem(key)

        def fn(e):
            for g in range(ngroups):
                e.dma_start(out=dst_3d[:, g, tcol:tcol + 128],
                            in_=src_bf_t[:, g * 128:(g + 1) * 128],
                            transpose=True).then_inc(sem, 16)
        ev = p.step("sync", fn, deps=[dep_src, self.prev_evac] + list(evac_deps),
                    inc=16 * ngroups, self_inc=True, key=key)
        self.prev_evac = ev
        return ev, ev


def emit_quant_e2m3(p, src_ap, dst_ap, scale_ap, tmp, tmp2, tmp3, width, dep, deps=()):
    w32 = width // 32

    def vec(v):
        r3 = lambda ap: ap.rearrange("p (b k) -> p b k", k=32)
        v.tensor_scalar(tmp[:, :width], src_ap, -1.0, None, ALU.mult)
        v.tensor_tensor(tmp[:, :width], tmp[:, :width], src_ap, ALU.max)
        v.tensor_reduce(scale_ap, r3(tmp[:, :width]), axis=AX.X, op=ALU.max)
        v.drain()
        v.tensor_scalar(scale_ap, scale_ap, 1.0 / 7.5, 1e-30, ALU.mult, ALU.max)
        v.drain()
        v.reciprocal(tmp3[:, :w32], scale_ap)
        v.drain()
        v.tensor_tensor(r3(tmp[:, :width]), r3(src_ap),
                        tmp3[:, :w32].unsqueeze(2).broadcast_to([128, w32, 32]), ALU.mult)
        v.tensor_scalar(tmp2[:, :width], tmp[:, :width], -1.0, None, ALU.mult)
        v.tensor_tensor(tmp2[:, :width], tmp2[:, :width], tmp[:, :width], ALU.max)
        v.tensor_scalar(tmp3[:, :width], tmp2[:, :width], 4.0, None, ALU.is_ge)
        v.tensor_scalar(tmp2[:, :width], tmp2[:, :width], 2.0, None, ALU.is_ge)
        v.tensor_scalar(tmp2[:, :width], tmp2[:, :width], -4.0, 8.0, ALU.mult, ALU.add)
        v.scalar_tensor_tensor(tmp2[:, :width], tmp3[:, :width], -2.0, tmp2[:, :width], ALU.mult, ALU.add)
        v.tensor_tensor(tmp[:, :width], tmp[:, :width], tmp2[:, :width], ALU.mult)
        v.tensor_scalar(tmp[:, :width], tmp[:, :width], MAGIC, -MAGIC, ALU.add, ALU.add)
        v.reciprocal(tmp2[:, :width], tmp2[:, :width])
        v.tensor_tensor(tmp[:, :width], tmp[:, :width], tmp2[:, :width], ALU.mult)
        return v.tensor_tensor(r3(dst_ap), r3(tmp[:, :width]),
                               scale_ap.unsqueeze(2).broadcast_to([128, w32, 32]), ALU.mult)
    return p.step("vector", vec, deps=[dep] + list(deps))



def build_l3ffn():
    nc = bass.Bass("TRN2", target_bir_lowering=False, debug=False)
    x = nc.dram_tensor("x", [TPC, D], F32, kind="ExternalInput").ap()
    w = nc.dram_tensor("w", [D, D], F32, kind="ExternalInput").ap()
    biasv = nc.dram_tensor("bias", [1, D], F32, kind="ExternalInput").ap()
    ident = nc.dram_tensor("ident", [128, 128], BF16, kind="ExternalInput").ap()
    y = nc.dram_tensor("y", [TPC, D], F32, kind="ExternalOutput").ap()

    with ExitStack() as es:
        sb = lambda name, shape, dt=F32: es.enter_context(nc.sbuf_tensor(name, shape, dt))
        pst_ = lambda name, shape, dt=F32: es.enter_context(nc.psum_tensor(name, shape, dt))
        xf = sb("xf", [128, NT, D]); sq = sb("sq", [128, D]); st = sb("st", [128, 4 * NT])
        xn = sb("xn", [128, 2, D], BF16); xnf = sb("xnf", [128, D])
        xqT = sb("xqT", [128, 8, TPC], BF16)
        wf = sb("wf", [128, 2, D]); wq = sb("wq", [128, 2, D], BF16); wqT = sb("wqT", [128, 8, D], BF16)
        bias_s = sb("bias_s", [1, D]); ones_s = sb("ones_s", [1, 128])
        idb = sb("idb", [128, 128], BF16)
        xsc = sb("xsc", [128, 32]); wsc = sb("wsc", [128, 8, 32])
        tmp = sb("tmp", [128, D]); tmp2 = sb("tmp2", [128, D]); tmp3 = sb("tmp3", [128, D])
        ps0 = pst_("ps0", [128, 512]); ps1 = pst_("ps1", [128, 512]); psT = pst_("psT", [128, 1024], BF16)
        p = es.enter_context(Prog(nc))
        ev_x = [dma_step(p, "gpsimd", [(xf[:, t, :], x[t * 128:(t + 1) * 128, :])]) for t in range(NT)]
        ev_misc = dma_step(p, "gpsimd", [(bias_s[:], biasv), (idb[:], ident)])
        ev_ones = p.step("vector", lambda v: v.memset(ones_s[:], 1.0))

        th = TransposeHelper(p, nc, psT, idb)
        th.prev_evac = ev_misc

        ev_wq_done = None
        ev_wq = []
        ev_wtp = {}
        for t in range(8):
            sl = t % 2
            dep_war = ev_wq[t - 2] if t >= 2 else None
            ev_wl = dma_step(p, "gpsimd", [(wf[:, sl, :], w[t * 128:(t + 1) * 128, :])], deps=[dep_war])
            ev_q = emit_quant_e2m3(p, wf[:, sl, :], wq[:, sl, :], wsc[:, t, :],
                                   tmp, tmp2, tmp3, D, ev_wl,
                                   deps=[ev_wtp[sl]] if sl in ev_wtp else [])
            ev_wq.append(ev_q)
            ev_wq_done, ev_wq_t = th.transpose_tile(wq[:, sl, :], wqT, t * 128, dep_src=ev_q)
            ev_wtp[sl] = ev_wq_t

        ev_xqT = []
        ev_xn_t = []
        for t in range(NT):
            sl = t % 2
            ev_ln = emit_ln(p, xf[:, t, :], xnf[:, :], st, sq, t, ev_x[t])
            dep_war = ev_xn_t[t - 2] if t >= 2 else None
            ev_q = emit_quant_e2m3(p, xnf[:, :], xn[:, sl, :], xsc[:, :], tmp, tmp2, tmp3, D, ev_ln,
                                   deps=[dep_war] if dep_war is not None else [])
            evv, evt = th.transpose_tile(xn[:, sl, :], xqT, t * 128, dep_src=ev_q)
            ev_xqT.append(evv)
            ev_xn_t.append(evt)

        ev_out = []
        for t in range(NT):
            for oc in range(2):
                pbank = ps0 if (t * 2 + oc) % 2 == 0 else ps1

                def tens(te, t=t, oc=oc, pbank=pbank):
                    nc.tensor.matmul(pbank[:, :],
                                     ones_s[:, 0:128],
                                     bias_s[:, oc * 512:(oc + 1) * 512],
                                     start=True, stop=False)
                    for g in range(8):
                        last = nc.tensor.matmul(pbank[:, :],
                                                xqT[:, g, t * 128:(t + 1) * 128],
                                                wqT[:, g, oc * 512:(oc + 1) * 512],
                                                start=False, stop=(g == 7))
                    return last
                deps = [ev_xqT[t], ev_wq_done, ev_ones, ev_misc]
                if len(ev_out) >= 2:
                    deps.append(ev_out[-2])
                ev_mm = p.step("tensor", tens, deps=deps)

                def vec(v, t=t, oc=oc, pbank=pbank):
                    return v.tensor_tensor(xf[:, t, oc * 512:(oc + 1) * 512], pbank[:, :],
                                           xf[:, t, oc * 512:(oc + 1) * 512], ALU.add)
                ev_out.append(p.step("vector", vec, deps=[ev_mm]))

        for t in range(NT):
            dma_step(p, "sync", [(y[t * 128:(t + 1) * 128, :], xf[:, t, :])],
                     deps=[ev_out[2 * t], ev_out[2 * t + 1]], exact=False)
        p.final_wait("sync")
        p.emit()
    return nc


_L3FFN_CACHE = {}


def run_l3ffn(x_full, weight, bias_full):
    if "nc" not in _L3FFN_CACHE:
        _L3FFN_CACHE["nc"] = build_l3ffn()
    nc = _L3FFN_CACHE["nc"]
    shards = split_tokens(x_full)
    bias = np.ascontiguousarray(np.asarray(bias_full, np.float32).reshape(1, D))
    w = np.ascontiguousarray(np.asarray(weight, np.float32))
    in_maps = [{"x": s, "w": w, "bias": bias, "ident": IDENT128} for s in shards]
    res = run_bass_kernel_spmd(nc, in_maps, core_ids=list(range(NCORES)))
    return join_tokens([r["y"] for r in res.results])



def build_l1ffn():
    nc = bass.Bass("TRN2", target_bir_lowering=False, debug=False)
    x = nc.dram_tensor("x", [TPC, D], F32, kind="ExternalInput").ap()
    win = nc.dram_tensor("win", [D, 64], F32, kind="ExternalInput").ap()
    binr = nc.dram_tensor("binr", [1, 64], F32, kind="ExternalInput").ap()
    w1r = nc.dram_tensor("w1r", [128, 1024], F32, kind="ExternalInput").ap()
    b1r = nc.dram_tensor("b1r", [128, 1024], F32, kind="ExternalInput").ap()
    w2r = nc.dram_tensor("w2r", [128, 1024], F32, kind="ExternalInput").ap()
    b2r = nc.dram_tensor("b2r", [128, 64], F32, kind="ExternalInput").ap()
    wout = nc.dram_tensor("wout", [64, D], F32, kind="ExternalInput").ap()
    boutr = nc.dram_tensor("boutr", [1, D], F32, kind="ExternalInput").ap()
    ident = nc.dram_tensor("ident", [128, 128], BF16, kind="ExternalInput").ap()
    y = nc.dram_tensor("y", [TPC, D], F32, kind="ExternalOutput").ap()

    with ExitStack() as es:
        sb = lambda name, shape, dt=F32: es.enter_context(nc.sbuf_tensor(name, shape, dt))
        pst_ = lambda name, shape, dt=F32: es.enter_context(nc.psum_tensor(name, shape, dt))
        xf = sb("xf", [128, NT, D]); sq = sb("sq", [128, D]); st = sb("st", [128, 4 * NT])
        xn = sb("xn", [128, 2, D], BF16)
        xnT = sb("xnT", [128, 8, TPC], BF16)
        win_s = sb("win_s", [128, 8, 64], BF16)
        bin_s = sb("bin_s", [1, 64]); bout_s = sb("bout_s", [1, D])
        w1s = sb("w1s", [128, 1024]); b1s = sb("b1s", [128, 1024])
        w2s = sb("w2s", [128, 1024]); b2s = sb("b2s", [128, 64])
        wout_s = sb("wout_s", [64, D], BF16)
        ones_s = sb("ones_s", [1, 128]); idb = sb("idb", [128, 128], BF16)
        sf = sb("sf", [128, 64])
        hh = sb("hh", [128, 1024]); hh2 = sb("hh2", [128, 1024])
        gf = sb("gf", [128, 64])
        gb = sb("gb", [128, 2, 128], BF16)
        gT = sb("gT", [128, TPC], BF16)
        ps_s = pst_("ps_s", [128, 64])
        ps0 = pst_("ps0", [128, 512]); ps1 = pst_("ps1", [128, 512])
        psT = pst_("psT", [128, 1024], BF16)
        p = es.enter_context(Prog(nc))

        ev_x = [dma_step(p, "gpsimd", [(xf[:, t, :], x[t * 128:(t + 1) * 128, :])]) for t in range(NT)]
        ev_wl = dma_step(p, "gpsimd", [
            *[(win_s[:, g, :], win[g * 128:(g + 1) * 128, :]) for g in range(8)],
            (bin_s[:], binr), (bout_s[:], boutr),
            (w1s[:], w1r), (b1s[:], b1r), (w2s[:], w2r), (b2s[:], b2r),
            (wout_s[:], wout), (idb[:], ident)])
        ev_ones = p.step("vector", lambda v: (v.memset(ones_s[:], 1.0),
                                              v.memset(gb.rearrange("p a b -> p (a b)"), 0.0))[-1])

        th = TransposeHelper(p, nc, psT, idb)
        th.prev_evac = ev_wl

        ev_gT_t = []
        ev_out = []
        ev_tp = {}
        for t in range(NT):
            sl = t % 2
            ev_ln = emit_ln(p, xf[:, t, :], xn[:, sl, :], st, sq, t, ev_x[t],
                            war_dep=ev_tp.get(sl))
            evv, evt = th.transpose_tile(xn[:, sl, :], xnT, t * 128, dep_src=ev_ln)
            ev_tp[sl] = evt
            def mm_s(te, t=t):
                nc.tensor.matmul(ps_s[:, :], ones_s[:, 0:128], bin_s[:, :], start=True, stop=False)
                for g in range(8):
                    last = nc.tensor.matmul(ps_s[:, :], xnT[:, g, t * 128:(t + 1) * 128],
                                            win_s[:, g, :], start=False, stop=(g == 7))
                return last
            deps = [evv, ev_wl, ev_ones]
            if t > 0:
                deps.append(ev_sf)
            ev_mm_s = p.step("tensor", mm_s, deps=deps)
            ev_sf = p.step("vector", lambda v: v.tensor_copy(sf[:, :], ps_s[:, :]), deps=[ev_mm_s])

            def vchain(v, sl=sl):
                v.drain()
                h3 = hh[:, :].rearrange("p (m i) -> p m i", i=16)
                v.tensor_tensor(h3, sf[:, :].unsqueeze(2).broadcast_to([128, 64, 16]),
                                w1s[:, :].rearrange("p (m i) -> p m i", i=16), ALU.mult)
                v.tensor_tensor(hh[:, :], hh[:, :], b1s[:, :], ALU.add)
                v.tensor_scalar(hh[:, :], hh[:, :], 0.0, None, ALU.max)
                v.tensor_tensor(hh2[:, :], hh[:, :], w2s[:, :], ALU.mult)
                v.tensor_reduce(gf[:, :], hh2[:, :].rearrange("p (m i) -> p m i", i=16), axis=AX.X, op=ALU.add)
                v.drain()
                v.tensor_tensor(gf[:, :], gf[:, :], b2s[:, :], ALU.add)
                v.drain()
                return v.tensor_copy(gb[:, sl, 0:64], gf[:, :])
            deps = [ev_sf]
            if t >= 2:
                deps.append(ev_gT_t[t - 2])
            ev_g = p.step("vector", vchain, deps=deps)

            ev_gT_t.append(None)
            xk = p.xkey(1)
            def tg2(e, t=t, sl=sl, xk=xk):
                e.dma_start(out=gT[:, t * 128:(t + 1) * 128], in_=gb[:, sl, :],
                            transpose=True).then_inc(p.sem(xk), 16)
            ev_t = p.step("sync", tg2, deps=[ev_g, ev_ones], inc=16, self_inc=True, key=xk)
            ev_gT_t[t] = ev_t
            ev_ev = ev_t

            for oc in range(2):
                pbank = ps0 if oc == 0 else ps1
                def mm_o(te, t=t, oc=oc, pbank=pbank):
                    nc.tensor.matmul(pbank[:, :], ones_s[:, 0:128],
                                     bout_s[:, oc * 512:(oc + 1) * 512], start=True, stop=False)
                    return nc.tensor.matmul(pbank[:, :], gT[0:64, t * 128:(t + 1) * 128],
                                            wout_s[0:64, oc * 512:(oc + 1) * 512], start=False, stop=True)
                deps = [ev_ev, ev_wl, ev_ones]
                if len(ev_out) >= 2:
                    deps.append(ev_out[-2])
                ev_mm = p.step("tensor", mm_o, deps=deps)
                def evac_o(v, t=t, oc=oc, pbank=pbank):
                    return v.tensor_tensor(xf[:, t, oc * 512:(oc + 1) * 512], pbank[:, :],
                                           xf[:, t, oc * 512:(oc + 1) * 512], ALU.add)
                ev_out.append(p.step("vector", evac_o, deps=[ev_mm]))

            dma_step(p, "sync", [(y[t * 128:(t + 1) * 128, :], xf[:, t, :])],
                     deps=[ev_out[-2], ev_out[-1]], exact=False)
        p.final_wait("sync")
        p.emit()
    return nc


_BUILD_CACHE = {}


def _cached(name, fn):
    if name not in _BUILD_CACHE:
        _BUILD_CACHE[name] = fn()
    return _BUILD_CACHE[name]


def run_l1ffn(x_full, Win, bin_, W1, b1, W2, b2, Wout, bout):
    nc = _cached("l1ffn", build_l1ffn)
    shards = split_tokens(x_full)
    f32 = np.float32
    reps = {
        "win": np.ascontiguousarray(np.asarray(Win, f32)),
        "binr": np.asarray(bin_, f32).reshape(1, 64),
        "w1r": np.ascontiguousarray(np.broadcast_to(np.asarray(W1, f32).reshape(-1), (128, 1024))),
        "b1r": np.ascontiguousarray(np.broadcast_to(np.asarray(b1, f32).reshape(-1), (128, 1024))),
        "w2r": np.ascontiguousarray(np.broadcast_to(np.asarray(W2, f32).reshape(-1), (128, 1024))),
        "b2r": np.ascontiguousarray(np.broadcast_to(np.asarray(b2, f32).reshape(64), (128, 64))),
        "wout": np.ascontiguousarray(np.asarray(Wout, f32)),
        "boutr": np.asarray(bout, f32).reshape(1, D),
        "ident": IDENT128,
    }
    in_maps = [dict(reps, x=s) for s in shards]
    res = run_bass_kernel_spmd(nc, in_maps, core_ids=list(range(NCORES)))
    return join_tokens([r["y"] for r in res.results])


L0FFN_NE = 32


def build_l0ffn():
    nc = bass.Bass("TRN2", target_bir_lowering=False, debug=False)
    x = nc.dram_tensor("x", [TPC, D], F32, kind="ExternalInput").ap()
    rwT = nc.dram_tensor("rwT", [D, 32], F32, kind="ExternalInput").ap()
    ew = nc.dram_tensor("ew", [32, D, D], F32, kind="ExternalInput").ap()
    ident = nc.dram_tensor("ident", [128, 128], BF16, kind="ExternalInput").ap()
    y = nc.dram_tensor("y", [TPC, D], F32, kind="ExternalOutput").ap()
    NE = L0FFN_NE

    with ExitStack() as es:
        sb = lambda name, shape, dt=F32: es.enter_context(nc.sbuf_tensor(name, shape, dt))
        pst_ = lambda name, shape, dt=F32: es.enter_context(nc.psum_tensor(name, shape, dt))
        xf = sb("xf", [128, NT, D]); sq = sb("sq", [128, D]); st = sb("st", [128, 4 * NT])
        ssc = sb("ssc", [128, D])
        xn = sb("xn", [128, 2, D], BF16)
        xnf = sb("xnf", [128, D])
        xlo = sb("xlo", [128, 2, D], BF16)
        xnTlo = sb("xnTlo", [128, 8, TPC], BF16)
        xnT = sb("xnT", [128, 8, TPC], BF16)
        rwT_s = sb("rwT_s", [128, 8, 32], BF16)
        rwT_lo = sb("rwT_lo", [128, 8, 32], BF16)
        rwT_f = sb("rwT_f", [128, 8, 32])
        we = sb("we", [128, 2, 8, D], BF16)
        wmf = sb("wmf", [128, NT, 32])
        rstat = sb("rstat", [128, 8])
        rwf = sb("rwf", [128, 32])
        idb = sb("idb", [128, 128], BF16)
        ps_r = pst_("ps_r", [128, 32])
        ps = [pst_(f"psb{i}", [128, 512]) for i in range(4)]
        psT = pst_("psT", [128, 1024], BF16)
        p = es.enter_context(Prog(nc))

        ev_x = [dma_step(p, "gpsimd", [(xf[:, t, :], x[t * 128:(t + 1) * 128, :])]) for t in range(NT)]
        ev_wl0 = dma_step(p, "gpsimd", [
            *[(rwT_f[:, g, :], rwT[g * 128:(g + 1) * 128, :]) for g in range(8)],
            (idb[:], ident)])

        rws_scr = sb("rws_scr", [128, rwT_f.shape[2] * 8])

        def vsplit(v):
            nE = rwT_f.shape[2] * 8
            v.tensor_copy(rwT_s.rearrange("p a b -> p (a b)"), rwT_f.rearrange("p a b -> p (a b)"))
            v.drain()
            v.tensor_copy(rws_scr[:, :], rwT_s.rearrange("p a b -> p (a b)"))
            v.drain()
            return v.tensor_tensor(rwT_lo.rearrange("p a b -> p (a b)"),
                                   rwT_f.rearrange("p a b -> p (a b)"), rws_scr[:, :], ALU.subtract)
        ev_wl = p.step("vector", vsplit, deps=[ev_wl0])

        th = TransposeHelper(p, nc, psT, idb)
        th.prev_evac = ev_wl

        ev_wm = []
        ev_router_free = None
        ev_sr_prev = None
        ev_tp = {}
        for t in range(NT):
            sl = t % 2
            ev_ln = emit_ln(p, xf[:, t, :], xn[:, sl, :], st, ssc, t, ev_x[t],
                            war_dep=ev_tp.get(sl), xn_f32=xnf[:, :])
            def vlo(v, sl=sl):
                v.tensor_copy(sq[:, :], xn[:, sl, :])
                v.drain()
                return v.tensor_tensor(xlo[:, sl, :], xnf[:, :], sq[:, :], ALU.subtract)
            ev_lo = p.step("vector", vlo, deps=[ev_ln, ev_tp.get(sl)])
            evv, evt = th.transpose_tile(xn[:, sl, :], xnT, t * 128, dep_src=ev_lo)
            evv2, evt2 = th.transpose_tile(xlo[:, sl, :], xnTlo, t * 128, dep_src=ev_lo)
            ev_tp[sl] = evt2
            def mm_r(te, t=t):
                for g in range(8):
                    nc.tensor.matmul(ps_r[:, :], xnT[:, g, t * 128:(t + 1) * 128],
                                     rwT_s[:, g, :], start=(g == 0), stop=False)
                for g in range(8):
                    nc.tensor.matmul(ps_r[:, :], xnT[:, g, t * 128:(t + 1) * 128],
                                     rwT_lo[:, g, :], start=False, stop=False)
                for g in range(8):
                    nc.tensor.matmul(ps_r[:, :], xnTlo[:, g, t * 128:(t + 1) * 128],
                                     rwT_s[:, g, :], start=False, stop=False)
                for g in range(8):
                    last = nc.tensor.matmul(ps_r[:, :], xnTlo[:, g, t * 128:(t + 1) * 128],
                                            rwT_lo[:, g, :], start=False, stop=(g == 7))
                return last
            deps = [evv, evv2, ev_wl]
            if ev_router_free is not None:
                deps.append(ev_router_free)
            ev_mr = p.step("tensor", mm_r, deps=deps)

            def vrouter(v, t=t):
                m1 = rstat[:, 0:1]; ssum = rstat[:, 1:2]; m2 = rstat[:, 2:3]
                v.tensor_reduce(m1, ps_r[:, :], axis=AX.X, op=ALU.max)
                v.drain()
                v.tensor_scalar(m1, m1, -1.0, None, ALU.mult)
                v.drain()
                return v.tensor_copy(rwf[:, :], ps_r[:, :])
            deps_vr = [ev_mr]
            if ev_sr_prev is not None:
                deps_vr.append(ev_sr_prev)
            ev_vr = p.step("vector", vrouter, deps=deps_vr)
            ev_router_free = ev_vr

            def srouter(s, t=t):
                return s.activation(rwf[:, :], rwf[:, :], AF.Exp, bias=rstat[:, 0:1],
                                    accum_out=rstat[:, 1:2])
            ev_sr = p.step("scalar", srouter, deps=[ev_vr])
            ev_sr_prev = ev_sr

            def vtop(v, t=t):
                m1 = rstat[:, 3:4]; m2 = rstat[:, 4:5]; rs = rstat[:, 5:6]
                v.reciprocal(rs, rstat[:, 1:2])
                v.drain()
                v.tensor_scalar(rwf[:, :], rwf[:, :], rs, None, ALU.mult)
                v.drain()
                v.tensor_reduce(m1, rwf[:, :], axis=AX.X, op=ALU.max)
                v.drain()
                v.tensor_scalar(wmf[:, t, :], rwf[:, :], m1, None, ALU.is_lt)
                v.drain()
                v.tensor_tensor(wmf[:, t, :], rwf[:, :], wmf[:, t, :], ALU.mult)
                v.drain()
                v.tensor_reduce(m2, wmf[:, t, :], axis=AX.X, op=ALU.max)
                v.drain()
                v.tensor_scalar(wmf[:, t, :], rwf[:, :], m2, None, ALU.is_ge)
                v.drain()
                return v.tensor_tensor(wmf[:, t, :], rwf[:, :], wmf[:, t, :], ALU.mult)
            ev_wm.append(p.step("vector", vtop, deps=[ev_sr]))

        ev_we = [None, None]
        ev_last_mm = [None, None]
        ev_evac = {}
        for e in range(NE):
            sl = e % 2
            dep = [ev_last_mm[sl]] if ev_last_mm[sl] is not None else []
            ev_we[sl] = dma_step(p, "gpsimd",
                                 [(we[:, sl, g, :], ew[e, g * 128:(g + 1) * 128, :]) for g in range(8)],
                                 deps=dep)
            for t in range(NT):
                for oc in range(2):
                    k = (t * 2 + oc) % 4
                    def mm_e(te, t=t, oc=oc, sl=sl, k=k):
                        for g in range(8):
                            last = nc.tensor.matmul(ps[k][:, :], xnT[:, g, t * 128:(t + 1) * 128],
                                                    we[:, sl, g, oc * 512:(oc + 1) * 512],
                                                    start=(g == 0), stop=(g == 7))
                        return last
                    deps = [ev_we[sl], ev_wm[t] if e == 0 else None]
                    if e == 0 and oc == 0:
                        deps.append(ev_wm[NT - 1])
                    prior = ev_evac.get(k)
                    if prior is not None:
                        deps.append(prior)
                    ev_mm = p.step("tensor", mm_e, deps=deps)
                    ev_last_mm[sl] = ev_mm

                    def evac(v, t=t, oc=oc, e=e, k=k):
                        return v.scalar_tensor_tensor(xf[:, t, oc * 512:(oc + 1) * 512],
                                                      ps[k][:, :], wmf[:, t, e:e + 1],
                                                      xf[:, t, oc * 512:(oc + 1) * 512],
                                                      ALU.mult, ALU.add)
                    ev_evac[k] = p.step("vector", evac, deps=[ev_mm, ev_wm[t]])

        for t in range(NT):
            dma_step(p, "sync", [(y[t * 128:(t + 1) * 128, :], xf[:, t, :])],
                     deps=[ev_evac[(t * 2) % 4], ev_evac[(t * 2 + 1) % 4],
                           p.last("vector")], exact=False)
        p.final_wait("sync")
        p.emit()
    return nc


def run_l0ffn(x_full, router_w, expert_weights):
    nc = _cached("l0ffn", build_l0ffn)
    shards = split_tokens(x_full)
    f32 = np.float32
    reps = {
        "rwT": np.ascontiguousarray(np.asarray(router_w, f32).T),
        "ew": np.ascontiguousarray(np.asarray(expert_weights, f32)),
        "ident": IDENT128,
    }
    in_maps = [dict(reps, x=s) for s in shards]
    res = run_bass_kernel_spmd(nc, in_maps, core_ids=list(range(NCORES)))
    return join_tokens([r["y"] for r in res.results])



def build_l2ffn():
    nc = bass.Bass("TRN2", target_bir_lowering=False, debug=False)
    x = nc.dram_tensor("x", [TPC, D], F32, kind="ExternalInput").ap()
    rwT = nc.dram_tensor("rwT", [D, 8], F32, kind="ExternalInput").ap()
    w1blk = nc.dram_tensor("w1blk", [8, 32, D, 128], F32, kind="ExternalInput").ap()
    w2T = nc.dram_tensor("w2T", [8, 4096, D], F32, kind="ExternalInput").ap()
    ident = nc.dram_tensor("ident", [128, 128], BF16, kind="ExternalInput").ap()
    y = nc.dram_tensor("y", [TPC, D], F32, kind="ExternalOutput").ap()
    NE = 8
    NBLK = 8
    TB = 256

    with ExitStack() as es:
        sb = lambda name, shape, dt=F32: es.enter_context(nc.sbuf_tensor(name, shape, dt))
        pst_ = lambda name, shape, dt=F32: es.enter_context(nc.psum_tensor(name, shape, dt))
        xf = sb("xf", [128, NT, D]); sq = sb("sq", [128, D]); st = sb("st", [128, 4 * NT])
        ssc = sb("ssc", [128, D])
        xn = sb("xn", [128, 2, D], BF16)
        xnf = sb("xnf", [128, D])
        xlo = sb("xlo", [128, 1, D], BF16)
        xnT = sb("xnT", [128, 8, TPC], BF16)
        rwT_s = sb("rwT_s", [128, 8, 8], BF16)
        rwT_lo = sb("rwT_lo", [128, 8, 8], BF16)
        rwT_f = sb("rwT_f", [128, 8, 8])
        we1 = sb("we1", [128, 2, 8, 128], BF16)
        w2s = sb("w2s", [128, 32, D], BF16)
        hT = sb("hT", [128, 32, TB], BF16)
        xnTlo = (w2s[:, :, :].rearrange("p a b -> p (a b)")[:, 0:8 * TPC]
                 .rearrange("p (g c) -> p g c", c=TPC))
        wmf = sb("wmf", [128, NT, 8])
        rstat = sb("rstat", [128, 8])
        rwf = sb("rwf", [128, 8])
        idb = sb("idb", [128, 128], BF16)
        ps_r = pst_("ps_r", [128, 8])
        psA = [pst_(f"psA{i}", [128, 512]) for i in range(2)]
        psB = [pst_(f"psB{i}", [128, 512]) for i in range(2)]
        psT = pst_("psT", [128, 1024], BF16)
        p = es.enter_context(Prog(nc))

        ev_x = [dma_step(p, "gpsimd", [(xf[:, t, :], x[t * 128:(t + 1) * 128, :])]) for t in range(NT)]
        ev_wl0 = dma_step(p, "gpsimd", [
            *[(rwT_f[:, g, :], rwT[g * 128:(g + 1) * 128, :]) for g in range(8)],
            (idb[:], ident)])

        rws_scr = sb("rws_scr", [128, rwT_f.shape[2] * 8])

        def vsplit(v):
            nE = rwT_f.shape[2] * 8
            v.tensor_copy(rwT_s.rearrange("p a b -> p (a b)"), rwT_f.rearrange("p a b -> p (a b)"))
            v.drain()
            v.tensor_copy(rws_scr[:, :], rwT_s.rearrange("p a b -> p (a b)"))
            v.drain()
            return v.tensor_tensor(rwT_lo.rearrange("p a b -> p (a b)"),
                                   rwT_f.rearrange("p a b -> p (a b)"), rws_scr[:, :], ALU.subtract)
        ev_wl = p.step("vector", vsplit, deps=[ev_wl0])

        th = TransposeHelper(p, nc, psT, idb)
        th.prev_evac = ev_wl

        ev_wm = []
        ev_router_free = None
        ev_sr_prev = None
        ev_xnT = []
        ev_tp = {}
        for t in range(NT):
            sl = t % 2
            ev_ln = emit_ln(p, xf[:, t, :], xn[:, sl, :], st, ssc, t, ev_x[t],
                            war_dep=ev_tp.get(sl), xn_f32=xnf[:, :])
            def vlo(v, sl=sl):
                v.tensor_copy(sq[:, :], xn[:, sl, :])
                v.drain()
                return v.tensor_tensor(xlo[:, 0, :], xnf[:, :], sq[:, :], ALU.subtract)
            ev_lo = p.step("vector", vlo, deps=[ev_ln, ev_tp.get(sl), ev_tp.get("lo")])
            evv, evt = th.transpose_tile(xn[:, sl, :], xnT, t * 128, dep_src=ev_lo)
            evv2, evt2 = th.transpose_tile(xlo[:, 0, :], xnTlo, t * 128, dep_src=ev_lo)
            ev_tp[sl] = evt
            ev_tp["lo"] = evt2
            ev_xnT.append(evv)

            def mm_r(te, t=t):
                for g in range(8):
                    nc.tensor.matmul(ps_r[:, :], xnT[:, g, t * 128:(t + 1) * 128],
                                     rwT_s[:, g, :], start=(g == 0), stop=False)
                for g in range(8):
                    nc.tensor.matmul(ps_r[:, :], xnT[:, g, t * 128:(t + 1) * 128],
                                     rwT_lo[:, g, :], start=False, stop=False)
                for g in range(8):
                    nc.tensor.matmul(ps_r[:, :], xnTlo[:, g, t * 128:(t + 1) * 128],
                                     rwT_s[:, g, :], start=False, stop=False)
                for g in range(8):
                    last = nc.tensor.matmul(ps_r[:, :], xnTlo[:, g, t * 128:(t + 1) * 128],
                                            rwT_lo[:, g, :], start=False, stop=(g == 7))
                return last
            deps = [evv, evv2, ev_wl]
            if ev_router_free is not None:
                deps.append(ev_router_free)
            ev_mr = p.step("tensor", mm_r, deps=deps)
            ev_mr_last = ev_mr

            def vrouter(v, t=t):
                m1 = rstat[:, 0:1]
                v.tensor_reduce(m1, ps_r[:, :], axis=AX.X, op=ALU.max)
                v.drain()
                v.tensor_scalar(m1, m1, -1.0, None, ALU.mult)
                v.drain()
                return v.tensor_copy(rwf[:, :], ps_r[:, :])
            deps_vr = [ev_mr]
            if ev_sr_prev is not None:
                deps_vr.append(ev_sr_prev)
            ev_vr = p.step("vector", vrouter, deps=deps_vr)
            ev_router_free = ev_vr

            def srouter(s, t=t):
                return s.activation(rwf[:, :], rwf[:, :], AF.Exp, bias=rstat[:, 0:1],
                                    accum_out=rstat[:, 1:2])
            ev_sr = p.step("scalar", srouter, deps=[ev_vr])
            ev_sr_prev = ev_sr

            def vtop(v, t=t):
                m1 = rstat[:, 3:4]; m2 = rstat[:, 4:5]; rs = rstat[:, 5:6]
                v.reciprocal(rs, rstat[:, 1:2])
                v.drain()
                v.tensor_scalar(rwf[:, :], rwf[:, :], rs, None, ALU.mult)
                v.drain()
                v.tensor_reduce(m1, rwf[:, :], axis=AX.X, op=ALU.max)
                v.drain()
                v.tensor_scalar(wmf[:, t, :], rwf[:, :], m1, None, ALU.is_lt)
                v.drain()
                v.tensor_tensor(wmf[:, t, :], rwf[:, :], wmf[:, t, :], ALU.mult)
                v.drain()
                v.tensor_reduce(m2, wmf[:, t, :], axis=AX.X, op=ALU.max)
                v.drain()
                v.tensor_scalar(wmf[:, t, :], rwf[:, :], m2, None, ALU.is_ge)
                v.drain()
                return v.tensor_tensor(wmf[:, t, :], rwf[:, :], wmf[:, t, :], ALU.mult)
            ev_wm.append(p.step("vector", vtop, deps=[ev_sr]))

        ev_w2 = None
        ev_lastB = None
        ev_we1 = [None, None]
        ev_lastA = [None, None]
        ev_evacA = {}
        ev_evacB = {}
        for e in range(NE):
            dep = [ev_lastB] if ev_lastB is not None else [ev_mr_last]
            ev_w2 = dma_step(p, "gpsimd",
                             [(w2s[:, :, :],
                               w2T[e].rearrange("(hb p) c -> p hb c", p=128))], deps=dep)
            for blk in range(NBLK):
                for ob in range(32):
                    slw = ob % 2
                    depw = [ev_lastA[slw]] if ev_lastA[slw] is not None else []
                    ev_we1[slw] = dma_step(p, "gpsimd",
                                           [(we1[:, slw, :, :],
                                             w1blk[e, ob].rearrange("(g p) c -> p g c", p=128))],
                                           deps=depw)
                    k = ob % 2
                    def mmA(te, e=e, blk=blk, ob=ob, slw=slw, k=k):
                        for g in range(8):
                            last = nc.tensor.matmul(psA[k][:, 0:TB],
                                                    we1[:, slw, g, :],
                                                    xnT[:, g, blk * TB:(blk + 1) * TB],
                                                    start=(g == 0), stop=(g == 7))
                        return last
                    deps = [ev_we1[slw]] + ([ev_evacA[k]] if k in ev_evacA else [])
                    if e == 0 and blk == 0:
                        deps += [ev_xnT[4 * blk + i] for i in range(4)]
                    if e == 0:
                        deps += [ev_xnT[min(4 * blk + 3, NT - 1)]]
                    ev_mmA = p.step("tensor", mmA, deps=deps)
                    ev_lastA[slw] = ev_mmA

                    def gelu_evac(s, blk=blk, ob=ob, k=k):
                        return s.activation(hT[:, ob, :], psA[k][:, 0:TB], AF.Gelu)
                    ev_evacA[k] = p.step("scalar", gelu_evac, deps=[ev_mmA, ev_lastB])
                for tt in range(2):
                    T = blk * 2 + tt
                    for oc in range(2):
                        k = (tt * 2 + oc) % 2
                        def mmB(te, T=T, tt=tt, oc=oc, k=k):
                            for hb in range(32):
                                last = nc.tensor.matmul(psB[k][:, :],
                                                        hT[:, hb, tt * 128:(tt + 1) * 128],
                                                        w2s[:, hb, oc * 512:(oc + 1) * 512],
                                                        start=(hb == 0), stop=(hb == 31))
                            return last
                        deps = [ev_evacA[0], ev_evacA[1], ev_w2] + ([ev_evacB[k]] if k in ev_evacB else [])
                        ev_mmB = p.step("tensor", mmB, deps=deps)
                        ev_lastB = ev_mmB

                        def evacB(v, T=T, oc=oc, e=e, k=k):
                            return v.scalar_tensor_tensor(xf[:, T, oc * 512:(oc + 1) * 512],
                                                          psB[k][:, :], wmf[:, T, e:e + 1],
                                                          xf[:, T, oc * 512:(oc + 1) * 512],
                                                          ALU.mult, ALU.add)
                        ev_evacB[k] = p.step("vector", evacB, deps=[ev_mmB, ev_wm[T]])

        for t in range(NT):
            dma_step(p, "sync", [(y[t * 128:(t + 1) * 128, :], xf[:, t, :])],
                     deps=[p.last("vector")], exact=False)
        p.final_wait("sync")
        p.emit()
    return nc


def run_l2ffn(x_full, router_w, w1, w2):
    nc = _cached("l2ffn", build_l2ffn)
    shards = split_tokens(x_full)
    f32 = np.float32
    w1 = np.asarray(w1, f32)
    w2 = np.asarray(w2, f32)
    w1blk = np.ascontiguousarray(
        w1.transpose(0, 2, 1).reshape(8, 1024, 32, 128).transpose(0, 2, 1, 3))
    reps = {
        "rwT": np.ascontiguousarray(np.asarray(router_w, f32).T),
        "w1blk": w1blk,
        "w2T": np.ascontiguousarray(w2.transpose(0, 2, 1)),
        "ident": IDENT128,
    }
    in_maps = [dict(reps, x=s) for s in shards]
    res = run_bass_kernel_spmd(nc, in_maps, core_ids=list(range(NCORES)))
    return join_tokens([r["y"] for r in res.results])



def emit_rope(v, t_ap, out_ap, scr_ap, ct_ap, st_ap, nh):
    t4 = t_ap.rearrange("p (h d) -> p h d", d=64)
    o4 = out_ap.rearrange("p (h d) -> p h d", d=64)
    s4 = scr_ap.rearrange("p (h d) -> p h d", d=64)
    t2 = t_ap.rearrange("p (h e d) -> p h e d", e=2, d=32)
    s2 = scr_ap.rearrange("p (h e d) -> p h e d", e=2, d=32)
    v.tensor_tensor(s2[:, :, 0, :], t2[:, :, 1, :],
                    st_ap[:, 0:32].unsqueeze(1).broadcast_to([128, nh, 32]), ALU.mult)
    v.tensor_tensor(s2[:, :, 1, :], t2[:, :, 0, :],
                    st_ap[:, 32:64].unsqueeze(1).broadcast_to([128, nh, 32]), ALU.mult)
    v.tensor_tensor(o4, t4, ct_ap.unsqueeze(1).broadcast_to([128, nh, 64]), ALU.mult)
    return v.tensor_tensor(o4, o4, s4, ALU.add)


PERM16 = [0, 4, 1, 5, 2, 6, 3, 7, 8, 12, 9, 13, 10, 14, 11, 15]



def build_l3attn():
    nc = bass.Bass("TRN2", target_bir_lowering=False, debug=False)
    xkv = nc.dram_tensor("xkv", [SEQ, D], F32, kind="ExternalInput").ap()
    xq = nc.dram_tensor("xq", [TPC, D], F32, kind="ExternalInput").ap()
    wq = nc.dram_tensor("wq", [D, D], F32, kind="ExternalInput").ap()
    wk = nc.dram_tensor("wk", [D, 256], F32, kind="ExternalInput").ap()
    wv = nc.dram_tensor("wv", [D, 256], F32, kind="ExternalInput").ap()
    wo = nc.dram_tensor("wo", [D, D], F32, kind="ExternalInput").ap()
    bq_i = nc.dram_tensor("bq", [1, D], F32, kind="ExternalInput").ap()
    bk_i = nc.dram_tensor("bk", [1, 256], F32, kind="ExternalInput").ap()
    bv_i = nc.dram_tensor("bv", [1, 256], F32, kind="ExternalInput").ap()
    bo_i = nc.dram_tensor("bo", [1, D], F32, kind="ExternalInput").ap()
    cos_q = nc.dram_tensor("cos_q", [TPC, 64], F32, kind="ExternalInput").ap()
    sin_q = nc.dram_tensor("sin_q", [TPC, 64], F32, kind="ExternalInput").ap()
    cos_k = nc.dram_tensor("cos_k", [SEQ, 64], F32, kind="ExternalInput").ap()
    sin_k = nc.dram_tensor("sin_k", [SEQ, 64], F32, kind="ExternalInput").ap()
    maskt = nc.dram_tensor("maskt", [16, 2, 128, 128], F32, kind="ExternalInput").ap()
    ident = nc.dram_tensor("ident", [128, 128], BF16, kind="ExternalInput").ap()
    y = nc.dram_tensor("y", [TPC, D], F32, kind="ExternalOutput").ap()
    M0 = 20.0
    NKT = SEQ // 128

    with ExitStack() as es:
        sb = lambda name, shape, dt=F32: es.enter_context(nc.sbuf_tensor(name, shape, dt))
        pst_ = lambda name, shape, dt=F32: es.enter_context(nc.psum_tensor(name, shape, dt))
        xf = sb("xf", [128, 4, D])
        sq = sb("sq", [128, D]); st = sb("st", [128, 4 * (NKT + NT)])
        xn = sb("xn", [128, 2, D], BF16)
        stg = sb("stg", [128, 2, 8, 128], BF16)
        qf = sb("qf", [128, D]); qf2 = sb("qf2", [128, D]); qf3 = sb("qf3", [128, D])
        qb = sb("qb", [128, 2, D], BF16)
        qT = sb("qT", [128, 8, TPC], BF16)
        kT = sb("kT", [128, 2, SEQ], BF16)
        vs = sb("vs", [128, NKT, 260], BF16)
        ao = sb("ao", [128, NT, D], BF16)
        ctq = sb("ctq", [128, NT, 64], BF16); stq = sb("stq", [128, NT, 64], BF16)
        ctk = sb("ctk", [128, NKT, 64], BF16); stk = sb("stk", [128, NKT, 64], BF16)
        msk = sb("msk", [128, 32, 128], BF16)
        wqs = sb("wqs", [128, 8, D], BF16)
        wks = sb("wks", [128, 8, 256], BF16)
        wvs = sb("wvs", [128, 8, 256], BF16)
        bqs = sb("bqs", [1, D]); bks = sb("bks", [1, 256]); bvs = sb("bvs", [1, 256]); bos = sb("bos", [1, D])
        ones_s = sb("ones_s", [1, 128])
        idb = sb("idb", [128, 128], BF16)
        pt = sb("pt", [128, 2, 512], BF16)
        rsb = sb("rsb", [128, 2])
        m0b = sb("m0b", [128, 1])
        psS = [pst_(f"psS{i}", [128, 512]) for i in range(2)]
        psA = [pst_(f"psA{i}", [128, 68]) for i in range(2)]
        psP = [pst_(f"psP{i}", [128, 512]) for i in range(2)]
        psT = pst_("psT", [128, 1024], BF16)
        p = es.enter_context(Prog(nc))

        ev_wl = dma_step(p, "gpsimd", [
            *[(wqs[:, g, :], wq[g * 128:(g + 1) * 128, :]) for g in range(8)],
            *[(wks[:, g, :], wk[g * 128:(g + 1) * 128, :]) for g in range(8)],
            *[(wvs[:, g, :], wv[g * 128:(g + 1) * 128, :]) for g in range(8)],
            (bqs[:], bq_i), (bks[:], bk_i), (bvs[:], bv_i), (bos[:], bo_i),
            (idb[:], ident),
            (msk[:, :, :], maskt.rearrange("s e p c -> p (s e) c")),
            (ctq[:, :, :], cos_q.rearrange("(t p) c -> p t c", p=128)),
            (stq[:, :, :], sin_q.rearrange("(t p) c -> p t c", p=128)),
            (ctk[:, :, :], cos_k.rearrange("(t p) c -> p t c", p=128)),
            (stk[:, :, :], sin_k.rearrange("(t p) c -> p t c", p=128)),
        ])
        ev_ones = p.step("vector", lambda v: (v.memset(ones_s[:], 1.0),
                                              v.memset(m0b[:], -M0),
                                              v.memset(vs.rearrange("p a b -> p (a b)"), 0.0))[-1])
        ev_vones = p.step("vector", lambda v: v.memset(
            vs[:, :, :].rearrange("p t (h u) -> p t h u", u=65)[:, :, :, 64:65], 1.0),
            deps=[ev_ones])

        th = TransposeHelper(p, nc, psT, idb)
        th.prev_evac = ev_wl

        ev_stage_free = {}
        ev_kT_done = []
        ev_vs_done = []
        ev_ppfree = {}
        for kt in range(NKT):
            slx = kt % 4
            ev_xl = dma_step(p, "gpsimd", [(xf[:, slx, :], xkv[kt * 128:(kt + 1) * 128, :])],
                             deps=[ev_stage_free.get(("ln", slx))])
            sl = kt % 2
            ev_ln = emit_ln(p, xf[:, slx, :], xn[:, sl, :], st, sq, kt, ev_xl,
                            war_dep=ev_stage_free.get(("xn", sl)))
            ev_stage_free[("ln", slx)] = ev_ln
            ss = kt % 2
            evv, evt = th.transpose_tile(xn[:, sl, :], stg[:, ss, :, :], 0, dep_src=ev_ln,
                                         evac_deps=[ev_stage_free.get(("stg", ss))])
            ev_stage_free[("xn", sl)] = evt
            kk = kt % 2
            def mmk(te, kt=kt, ss=ss, kk=kk):
                nc.tensor.matmul(psP[kk][:, 0:256], ones_s[:, 0:128], bks[:, :], start=True, stop=False)
                for g in range(8):
                    nc.tensor.matmul(psP[kk][:, 0:256], stg[:, ss, g, :], wks[:, g, :],
                                     start=False, stop=(g == 7))
                nc.tensor.matmul(psP[kk][:, 256:512], ones_s[:, 0:128], bvs[:, :], start=True, stop=False)
                for g in range(8):
                    last = nc.tensor.matmul(psP[kk][:, 256:512], stg[:, ss, g, :], wvs[:, g, :],
                                            start=False, stop=(g == 7))
                return last
            deps = [evv, ev_wl, ev_ones, ev_ppfree.get(kk)]
            ev_mk = p.step("tensor", mmk, deps=deps)
            ev_stage_free[("stg", ss)] = ev_mk

            def vk(v, kt=kt, kk=kk):
                v.tensor_copy(qf[:, 0:256], psP[kk][:, 0:256])
                v.drain()
                emit_rope(v, qf[:, 0:256], qf2[:, 0:256], qf3[:, 0:256],
                          ctk[:, kt, :], stk[:, kt, :], 4)
                v.drain()
                v.tensor_copy(qb[:, 0, 0:256], qf2[:, 0:256])
                return v.tensor_copy(
                    vs[:, kt, :].rearrange("p (h u) -> p h u", u=65)[:, :, 0:64],
                    psP[kk][:, 256:512].rearrange("p (h u) -> p h u", u=64))
            ev_vk = p.step("vector", vk, deps=[ev_mk, ev_vones, ev_kT_done[-1] if ev_kT_done else None])
            ev_vs_done.append(ev_vk)
            ev_ppfree[kk] = ev_vk

            xk = p.xkey(1)
            def tk(e, kt=kt, xk=xk):
                for b2 in range(2):
                    e.dma_start(out=kT[:, b2, kt * 128:(kt + 1) * 128],
                                in_=qb[:, 0, b2 * 128:(b2 + 1) * 128],
                                transpose=True).then_inc(p.sem(xk), 16)
            ev_ek = p.step("sync", tk, deps=[ev_vk], inc=32, self_inc=True, key=xk)
            ev_kT_done.append(ev_ek)

        ev_qT_done = []
        for s in range(NT):
            slx = s % 4
            ev_xl = dma_step(p, "gpsimd", [(xf[:, slx, :], xq[s * 128:(s + 1) * 128, :])],
                             deps=[ev_stage_free.get(("ln", slx))])
            sl = s % 2
            ev_ln = emit_ln(p, xf[:, slx, :], xn[:, sl, :], st, sq, NKT + s, ev_xl,
                            war_dep=ev_stage_free.get(("xn", sl)))
            ev_stage_free[("ln", slx)] = ev_ln
            ss = s % 2
            evv, evt = th.transpose_tile(xn[:, sl, :], stg[:, ss, :, :], 0, dep_src=ev_ln,
                                         evac_deps=[ev_stage_free.get(("stg", ss))])
            ev_stage_free[("xn", sl)] = evt
            def mmq2(te, s=s, ss=ss):
                for oc in range(2):
                    nc.tensor.matmul(psP[oc][:, :], ones_s[:, 0:128],
                                     bqs[:, oc * 512:(oc + 1) * 512], start=True, stop=False)
                    for g in range(8):
                        last = nc.tensor.matmul(psP[oc][:, :], stg[:, ss, g, :],
                                                wqs[:, g, oc * 512:(oc + 1) * 512],
                                                start=False, stop=(g == 7))
                return last
            deps = [evv, ev_wl, ev_ones, ev_ppfree.get(0), ev_ppfree.get(1)]
            ev_mq = p.step("tensor", mmq2, deps=deps)
            ev_stage_free[("stg", ss)] = ev_mq
            ev_mq_last = ev_mq

            def vq(v, s=s):
                v.tensor_copy(qf[:, 0:512], psP[0][:, :])
                v.tensor_copy(qf[:, 512:1024], psP[1][:, :])
                v.drain()
                emit_rope(v, qf[:, :], qf2[:, :], qf3[:, :], ctq[:, s, :], stq[:, s, :], 16)
                v.drain()
                return v.tensor_copy(qb[:, 1, :], qf2[:, :])
            ev_vq = p.step("vector", vq, deps=[ev_mq, ev_qT_done[-1] if ev_qT_done else None])
            ev_ppfree[0] = ev_vq
            ev_ppfree[1] = ev_vq

            xk = p.xkey(1)
            def tq(e, s=s, xk=xk):
                for b2 in range(8):
                    e.dma_start(out=qT[:, b2, s * 128:(s + 1) * 128],
                                in_=qb[:, 1, b2 * 128:(b2 + 1) * 128],
                                transpose=True).then_inc(p.sem(xk), 16)
            ev_eq = p.step("sync", tq, deps=[ev_vq], inc=16 * 8, self_inc=True, key=xk)
            ev_qT_done.append(ev_eq)

        ev_kv_all = [ev_kT_done[-1], ev_vs_done[-1]]

        ev_ptfree = {}
        ev_psSfree = {}
        ev_psAfree = {}
        ev_att_out = []
        for g in range(8):
            for e in range(2):
                kvh = 2 * (g // 4) + e
                pos = 2 * g + e
                for s in range(NT):
                    nch = 2 * s + 2
                    ngrp = (nch + 3) // 4
                    av = s % 2
                    ev_avdeps = [ev_psAfree.get(av)]
                    first_av = True
                    ev_exp = None
                    for gi in range(ngrp):
                        c0 = gi * 4
                        ncur = min(4, nch - c0)
                        bk = gi % 2
                        ptk = gi % 2
                        def mmS(te, g=g, e=e, s=s, c0=c0, ncur=ncur, bk=bk):
                            for ci in range(ncur):
                                j = c0 + ci
                                last = nc.tensor.matmul(
                                    psS[bk][:, ci * 128:(ci + 1) * 128],
                                    kT[64 * e:64 * e + 64, g // 4, j * 128:(j + 1) * 128],
                                    qT[64 * e:64 * e + 64, g, s * 128:(s + 1) * 128],
                                    start=True, stop=True)
                            return last
                        deps = [ev_kv_all[0], ev_kv_all[1], ev_qT_done[s], ev_psSfree.get(bk)]
                        ev_mS = p.step("tensor", mmS, deps=deps)

                        ev_m = ev_mS
                        if c0 + ncur >= nch - 1:
                            def vmask(v, s=s, c0=c0, ncur=ncur, bk=bk):
                                last = None
                                for ci in range(ncur):
                                    j = c0 + ci
                                    if j >= 2 * s:
                                        last = v.scalar_tensor_tensor(
                                            psS[bk][:, ci * 128:(ci + 1) * 128],
                                            msk[:, j, :], 3.0e38,
                                            psS[bk][:, ci * 128:(ci + 1) * 128],
                                            ALU.mult, ALU.add)
                                return last
                            ev_m = p.step("vector", vmask, deps=[ev_mS, ev_wl])

                        def sexp(sc, ncur=ncur, bk=bk, ptk=ptk):
                            return sc.activation(pt[:, ptk, 0:ncur * 128], psS[bk][:, 0:ncur * 128],
                                                 AF.Exp, bias=m0b[:, :])
                        ev_exp = p.step("scalar", sexp, deps=[ev_m, ev_ptfree.get(ptk), ev_ones])
                        ev_psSfree[bk] = ev_exp

                        def mmA(te, g=g, e=e, s=s, c0=c0, ncur=ncur, ptk=ptk, av=av,
                                first=first_av, kvh=kvh, nch=nch):
                            for ci in range(ncur):
                                j = c0 + ci
                                last = nc.tensor.matmul(
                                    psA[av][:, 0:65],
                                    pt[:, ptk, ci * 128:(ci + 1) * 128],
                                    vs[:, j, 65 * kvh:65 * kvh + 65],
                                    start=(first and ci == 0), stop=(j == nch - 1))
                            return last
                        ev_mA = p.step("tensor", mmA, deps=[ev_exp] + ([ev_avdeps[0]] if first_av and ev_avdeps[0] else []))
                        ev_ptfree[ptk] = ev_mA
                        ev_mA_last = ev_mA
                        first_av = False
                    def vev(v, pos=pos, s=s, av=av):
                        v.tensor_scalar(rsb[:, 0:1], psA[av][:, 64:65], 1e-30, None, ALU.add)
                        v.drain()
                        v.reciprocal(rsb[:, 0:1], rsb[:, 0:1])
                        v.drain()
                        return v.tensor_scalar(ao[:, s, 64 * pos:64 * pos + 64],
                                               psA[av][:, 0:64], rsb[:, 0:1], None, ALU.mult)
                    ev_psAfree[av] = p.step("vector", vev, deps=[ev_mA])
                    ev_att_out.append(ev_psAfree[av])

        ev_wo = dma_step(p, "gpsimd",
                         [(wqs[:, g, :], wo[g * 128:(g + 1) * 128, :]) for g in range(8)],
                         deps=[ev_mq_last])
        aoT = qT
        ev_out = []
        ev_ydma = []
        for s in range(NT):
            evv, evt = th.transpose_tile(ao[:, s, :], aoT, s * 128,
                                         dep_src=ev_att_out[-1],
                                         evac_deps=[ev_mA_last])
            ev_xr = dma_step(p, "gpsimd", [(xf[:, s % 4, :], xq[s * 128:(s + 1) * 128, :])],
                             deps=[ev_ydma[s - 4] if s >= 4 else ev_mq_last])
            for oc in range(2):
                kk = oc
                def mmo(te, s=s, oc=oc, kk=kk):
                    nc.tensor.matmul(psP[kk][:, :], ones_s[:, 0:128],
                                     bos[:, oc * 512:(oc + 1) * 512], start=True, stop=False)
                    for g in range(8):
                        last = nc.tensor.matmul(psP[kk][:, :], aoT[:, g, s * 128:(s + 1) * 128],
                                                wqs[:, g, oc * 512:(oc + 1) * 512],
                                                start=False, stop=(g == 7))
                    return last
                deps = [evv, ev_wo, ev_out[-2] if len(ev_out) >= 2 else None]
                ev_mo = p.step("tensor", mmo, deps=deps)
                def veo(v, s=s, oc=oc, kk=kk):
                    return v.tensor_tensor(xf[:, s % 4, oc * 512:(oc + 1) * 512], psP[kk][:, :],
                                           xf[:, s % 4, oc * 512:(oc + 1) * 512], ALU.add)
                ev_out.append(p.step("vector", veo, deps=[ev_mo, ev_xr]))
            ev_ydma.append(dma_step(p, "sync", [(y[s * 128:(s + 1) * 128, :], xf[:, s % 4, :])],
                           deps=[ev_out[-2], ev_out[-1]], exact=False))
        p.final_wait("sync")
        p.emit()
    return nc


def _rope_tables(pos, scale=1.0):
    inv = 1.0 / (10000.0 ** (np.arange(0, 64, 2, dtype=np.float32) / 64.0))
    ang = pos[:, None].astype(np.float32) * inv[None]
    c = np.cos(ang); s = np.sin(ang)
    cos64 = np.concatenate([c, c], -1) * scale
    sinS = np.concatenate([-s, s], -1) * scale
    return cos64.astype(np.float32), sinS.astype(np.float32)


def run_l3attn(x_full, Wq, bq, Wk, bk, Wv, bv, Wo, bo):
    nc = _cached("l3attn", build_l3attn)
    f32 = np.float32
    x_full = np.asarray(x_full, f32)
    Wq = np.asarray(Wq, f32); Wo = np.asarray(Wo, f32)
    bq = np.asarray(bq, f32)
    colperm = np.concatenate([np.arange(64 * h, 64 * h + 64) for h in PERM16])
    wq_p = np.ascontiguousarray(Wq[:, colperm])
    bq_p = np.ascontiguousarray(bq[colperm]).reshape(1, D)
    wo_p = np.ascontiguousarray(Wo[colperm, :])
    pos_k = np.arange(SEQ)
    ck, sk = _rope_tables(pos_k)
    in_maps = []
    for c in range(NCORES):
        b, pr = c // 2, c % 2
        G = [2 * s + pr for s in range(NT)]
        xb = x_full[b]
        xq_rows = np.ascontiguousarray(
            np.concatenate([xb[g * 128:(g + 1) * 128] for g in G], 0))
        pos_q = np.concatenate([np.arange(g * 128, (g + 1) * 128) for g in G])
        cq, sq_ = _rope_tables(pos_q, scale=0.125)
        masks = np.zeros((16, 2, 128, 128), f32)
        pp = np.arange(128)[:, None]
        cc = np.arange(128)[None, :]
        for s in range(NT):
            for e2 in range(2):
                allowed = (2 * s + e2) * 128 + pp <= (2 * s + pr) * 128 + cc
                masks[s, e2] = np.where(allowed, 0.0, -1.0).astype(f32)
        in_maps.append({
            "xkv": np.ascontiguousarray(xb), "xq": xq_rows,
            "wq": wq_p, "wk": np.ascontiguousarray(np.asarray(Wk, f32)),
            "wv": np.ascontiguousarray(np.asarray(Wv, f32)), "wo": wo_p,
            "bq": bq_p, "bk": np.asarray(bk, f32).reshape(1, 256),
            "bv": np.asarray(bv, f32).reshape(1, 256), "bo": np.asarray(bo, f32).reshape(1, D),
            "cos_q": cq, "sin_q": sq_, "cos_k": ck, "sin_k": sk,
            "maskt": masks, "ident": IDENT128,
        })
    res = run_bass_kernel_spmd(nc, in_maps, core_ids=list(range(NCORES)))
    out = np.zeros((4, SEQ, D), f32)
    for c in range(NCORES):
        b, pr = c // 2, c % 2
        yv = res.results[c]["y"]
        for s in range(NT):
            g = 2 * s + pr
            out[b, g * 128:(g + 1) * 128] = yv[s * 128:(s + 1) * 128]
    return out



def build_l0attn():
    nc = bass.Bass("TRN2", target_bir_lowering=False, debug=False)
    xkv = nc.dram_tensor("xkv", [SEQ, D], F32, kind="ExternalInput").ap()
    xq = nc.dram_tensor("xq", [TPC, D], F32, kind="ExternalInput").ap()
    wqw = nc.dram_tensor("wqw", [D, D], F32, kind="ExternalInput").ap()
    wkv = nc.dram_tensor("wkv", [D, 2048], F32, kind="ExternalInput").ap()
    wout = nc.dram_tensor("wout", [D, D], F32, kind="ExternalInput").ap()
    normg = nc.dram_tensor("normg", [128, D], F32, kind="ExternalInput").ap()
    normb = nc.dram_tensor("normb", [128, D], F32, kind="ExternalInput").ap()
    qsr_i = nc.dram_tensor("qsr", [128, 64], F32, kind="ExternalInput").ap()
    ksr_i = nc.dram_tensor("ksr", [128, 64], F32, kind="ExternalInput").ap()
    nullk = nc.dram_tensor("nullk", [2, D], F32, kind="ExternalInput").ap()
    nullv = nc.dram_tensor("nullv", [2, D], F32, kind="ExternalInput").ap()
    jcrow = nc.dram_tensor("jcrow", [1, 4224], F32, kind="ExternalInput").ap()
    iq2row = nc.dram_tensor("iq2row", [1, TPC], F32, kind="ExternalInput").ap()
    spos = nc.dram_tensor("spos", [1, 16 * 128], F32, kind="ExternalInput").ap()
    maskt = nc.dram_tensor("maskt", [16, 2, 128, 128], F32, kind="ExternalInput").ap()
    ident = nc.dram_tensor("ident", [128, 128], BF16, kind="ExternalInput").ap()
    y = nc.dram_tensor("y", [TPC, D], F32, kind="ExternalOutput").ap()
    NKT = SEQ // 128
    NCH = 33

    with ExitStack() as es:
        sb = lambda name, shape, dt=F32: es.enter_context(nc.sbuf_tensor(name, shape, dt))
        pst_ = lambda name, shape, dt=F32: es.enter_context(nc.psum_tensor(name, shape, dt))
        xf = sb("xf", [128, 4, D])
        sq = sb("sq", [128, D]); st = sb("st", [128, 4 * 64])
        xn = sb("xn", [128, 2, D], BF16)
        stg = sb("stg", [128, 2, 8, 128], BF16)
        qf = sb("qf", [128, D]); qf2 = sb("qf2", [128, D]); qf3 = sb("qf3", [128, D])
        qb = sb("qb", [128, 2, D], BF16)
        qT = sb("qT", [128, 2, TPC], BF16)
        kT = sb("kT", [128, 2, NCH * 128], BF16)
        vs = sb("vs", [128, NCH, 260], BF16)
        ao = sb("ao", [128, NT, D], BF16)
        gr = sb("gr", [128, D]); br = sb("br", [128, D])
        qsr = sb("qsr_s", [128, 64]); ksr = sb("ksr_s", [128, 64])
        nkt = sb("nkt", [128, D], BF16)
        nrm = sb("nrm", [128, 20])
        msk = sb("msk", [128, 32, 128], BF16)
        wq_s = sb("wq_s", [128, 8, 256], BF16)
        wkv_s = sb("wkv_s", [128, 8, 512], BF16)
        wo_s = sb("wo_s", [128, 8, D], BF16)
        jcr = sb("jcr", [1, 4224]); iqr = sb("iqr", [1, TPC])
        sps = sb("sps", [1, 2048])
        ones_s = sb("ones_s", [1, 128])
        idb = sb("idb", [128, 128], BF16)
        pt = sb("pt", [128, 2, 512], BF16)
        rsb = sb("rsb", [128, 2])
        m0b = sb("m0b", [128, 1])
        psS = [pst_(f"psS{i}", [128, 512]) for i in range(2)]
        psA = [pst_(f"psA{i}", [128, 68]) for i in range(2)]
        psP = [pst_(f"psP{i}", [128, 512]) for i in range(2)]
        psT = pst_("psT", [128, 1024], BF16)
        p = es.enter_context(Prog(nc))

        ev_wl = dma_step(p, "gpsimd", [
            (gr[:, :], normg), (br[:, :], normb),
            (qsr[:, :], qsr_i), (ksr[:, :], ksr_i),
            (idb[:], ident),
            (msk[:, :, :], maskt.rearrange("s e p c -> p (s e) c")),
            (jcr[:, :], jcrow), (iqr[:, :], iq2row),
            (sps[:, :], spos),
            (wo_s[:, :, :], wout.rearrange("(g p) c -> p g c", p=128)),
        ])
        def vinit(v):
            v.memset(ones_s[:], 1.0)
            v.memset(vs.rearrange("p a b -> p (a b)"), 0.0)
            v.memset(vs[:, :, :].rearrange("p t (h u) -> p t h u", u=65)[:, :, :, 64:65], 1.0)
            v.memset(nkt[:, :], 0.0)
            a = nrm[:, 16:17]; b = nrm[:, 17:18]; t1 = nrm[:, 18:19]
            v.tensor_scalar(qf[:, 0:64], qsr[:, :], -1.0, None, ALU.mult)
            v.tensor_tensor(qf[:, 0:64], qf[:, 0:64], qsr[:, :], ALU.max)
            v.drain()
            v.tensor_reduce(a, qf[:, 0:64], axis=AX.X, op=ALU.max)
            v.drain()
            v.tensor_scalar(qf[:, 0:64], ksr[:, :], -1.0, None, ALU.mult)
            v.tensor_tensor(qf[:, 0:64], qf[:, 0:64], ksr[:, :], ALU.max)
            v.drain()
            v.tensor_reduce(b, qf[:, 0:64], axis=AX.X, op=ALU.max)
            v.drain()
            v.tensor_tensor(t1, a, b, ALU.mult)
            v.drain()
            return v.tensor_scalar(m0b[:, :], t1, -1.0, None, ALU.mult)
        ev_init = p.step("vector", vinit, deps=[ev_wl])

        th = TransposeHelper(p, nc, psT, idb)
        th.prev_evac = ev_init

        ev_nk = dma_step(p, "gpsimd", [(qf[0:2, :], nullk)], deps=[ev_init])
        def vnull(v):
            q3 = qf[0:2, :].rearrange("p (h d) -> p h d", d=64)
            v.tensor_tensor(qf3[0:2, :].rearrange("p (h d) -> p h d", d=64), q3, q3, ALU.mult)
            v.drain()
            return v.tensor_reduce(nrm[0:2, 0:16], qf3[0:2, :].rearrange("p (h d) -> p h d", d=64),
                                   axis=AX.X, op=ALU.add)
        ev_vn = p.step("vector", vnull, deps=[ev_nk])
        ev_sn = p.step("scalar", lambda s: s.activation(nrm[0:2, 0:16], nrm[0:2, 0:16], AF.Sqrt),
                       deps=[ev_vn])
        def vnull2(v):
            v.tensor_scalar(nrm[0:2, 0:16], nrm[0:2, 0:16], 1e-12, None, ALU.max)
            v.drain()
            v.reciprocal(nrm[0:2, 0:16], nrm[0:2, 0:16])
            v.drain()
            q3 = qf[0:2, :].rearrange("p (h d) -> p h d", d=64)
            v.tensor_tensor(qf2[0:2, :].rearrange("p (h d) -> p h d", d=64), q3,
                            nrm[0:2, 0:16].unsqueeze(2).broadcast_to([2, 16, 64]), ALU.mult)
            v.drain()
            return v.tensor_tensor(nkt[0:2, :].rearrange("p (h d) -> p h d", d=64),
                                   qf2[0:2, :].rearrange("p (h d) -> p h d", d=64),
                                   ksr[0:2, :].unsqueeze(1).broadcast_to([2, 16, 64]), ALU.mult)
        ev_nkt = p.step("vector", vnull2, deps=[ev_sn])


        ev_ydma = []
        ev_attn_done_prev = None
        ev_stg_last = {}
        for hg in range(4):
            depw = [ev_attn_done_prev] if ev_attn_done_prev is not None else []
            ev_wg = dma_step(p, "gpsimd", [
                *[(wq_s[:, g, :], wqw[g * 128:(g + 1) * 128, 256 * hg:256 * (hg + 1)]) for g in range(8)],
                *[(wkv_s[:, g, 0:256], wkv[g * 128:(g + 1) * 128, 256 * hg:256 * (hg + 1)]) for g in range(8)],
                *[(wkv_s[:, g, 256:512], wkv[g * 128:(g + 1) * 128, 1024 + 256 * hg:1024 + 256 * (hg + 1)]) for g in range(8)],
            ], deps=depw)

            xk = p.xkey(1)
            def tnk(e, hg=hg, xk=xk):
                for b2 in range(2):
                    e.dma_start(out=kT[:, b2, 32 * 128:33 * 128],
                                in_=nkt[:, 256 * hg + b2 * 128:256 * hg + (b2 + 1) * 128],
                                transpose=True).then_inc(p.sem(xk), 16)
            ev_enk = p.step("sync", tnk, deps=[ev_nkt, ev_attn_done_prev], inc=32, self_inc=True, key=xk)
            ev_nv = dma_step(p, "gpsimd", [
                (vs[0:2, 32, :].rearrange("p (h u) -> p h u", u=65)[:, :, 0:64],
                 nullv[:, 256 * hg:256 * (hg + 1)].rearrange("p (h u) -> p h u", u=64)),
            ], deps=depw + [ev_init])

            ev_stage_free = {}
            ev_ppfree = {}
            ev_kT_done = ev_enk
            ev_vs_done = ev_nv
            ev_stage_free[("stg", 0)] = ev_stg_last.get(0)
            ev_stage_free[("stg", 1)] = ev_stg_last.get(1)
            for kt in range(NKT):
                slx = kt % 4
                ev_xl = dma_step(p, "gpsimd", [(xf[:, slx, :], xkv[kt * 128:(kt + 1) * 128, :])],
                                 deps=[ev_stage_free.get(("ln", slx))])
                sl = kt % 2
                ev_ln = emit_ln(p, xf[:, slx, :], xn[:, sl, :], st, sq, kt, ev_xl,
                                war_dep=ev_stage_free.get(("xn", sl)))
                ev_stage_free[("ln", slx)] = ev_ln
                ss = kt % 2
                evv, evt = th.transpose_tile(xn[:, sl, :], stg[:, ss, :, :], 0, dep_src=ev_ln,
                                             evac_deps=[ev_stage_free.get(("stg", ss))])
                ev_stage_free[("xn", sl)] = evt
                kk = kt % 2
                def mmkv(te, ss=ss, kk=kk):
                    for g in range(8):
                        last = nc.tensor.matmul(psP[kk][:, :], stg[:, ss, g, :], wkv_s[:, g, :],
                                                start=(g == 0), stop=(g == 7))
                    return last
                deps = [evv, ev_wg, ev_init, ev_ppfree.get(kk)]
                ev_mk = p.step("tensor", mmkv, deps=deps)
                ev_stage_free[("stg", ss)] = ev_mk

                def vkv1(v, kk=kk):
                    v.tensor_copy(qf[:, 0:256], psP[kk][:, 0:256])
                    v.drain()
                    q3 = qf[:, 0:256].rearrange("p (h d) -> p h d", d=64)
                    v.tensor_tensor(sq[:, 0:256].rearrange("p (h d) -> p h d", d=64), q3, q3, ALU.mult)
                    v.drain()
                    return v.tensor_reduce(nrm[:, 0:4], sq[:, 0:256].rearrange("p (h d) -> p h d", d=64),
                                           axis=AX.X, op=ALU.add)
                ev_v1 = p.step("vector", vkv1, deps=[ev_mk])
                ev_s1 = p.step("scalar", lambda s: s.activation(nrm[:, 0:4], nrm[:, 0:4], AF.Sqrt),
                               deps=[ev_v1])
                def vkv2(v, kt=kt, kk=kk):
                    v.tensor_scalar(nrm[:, 0:4], nrm[:, 0:4], 1e-12, None, ALU.max)
                    v.drain()
                    v.reciprocal(nrm[:, 0:4], nrm[:, 0:4])
                    v.drain()
                    q3 = qf[:, 0:256].rearrange("p (h d) -> p h d", d=64)
                    v.tensor_tensor(qf2[:, 0:256].rearrange("p (h d) -> p h d", d=64), q3,
                                    nrm[:, 0:4].unsqueeze(2).broadcast_to([128, 4, 64]), ALU.mult)
                    v.drain()
                    v.tensor_tensor(qb[:, 0, 0:256].rearrange("p (h d) -> p h d", d=64),
                                    qf2[:, 0:256].rearrange("p (h d) -> p h d", d=64),
                                    ksr[:, :].unsqueeze(1).broadcast_to([128, 4, 64]), ALU.mult)
                    return v.tensor_copy(
                        vs[:, kt, :].rearrange("p (h u) -> p h u", u=65)[:, :, 0:64],
                        psP[kk][:, 256:512].rearrange("p (h u) -> p h u", u=64))
                ev_v2 = p.step("vector", vkv2, deps=[ev_s1, ev_kT_done])
                ev_vs_done = ev_v2
                ev_ppfree[kk] = ev_v2

                xk = p.xkey(1)
                def tk(e, kt=kt, xk=xk):
                    for b2 in range(2):
                        e.dma_start(out=kT[:, b2, kt * 128:(kt + 1) * 128],
                                    in_=qb[:, 0, b2 * 128:(b2 + 1) * 128],
                                    transpose=True).then_inc(p.sem(xk), 16)
                ev_ek = p.step("sync", tk, deps=[ev_v2], inc=32, self_inc=True, key=xk)
                ev_kT_done = ev_ek

            ev_qT_done = []
            for s in range(NT):
                slx = s % 4
                ev_xl = dma_step(p, "gpsimd", [(xf[:, slx, :], xq[s * 128:(s + 1) * 128, :])],
                                 deps=[ev_stage_free.get(("ln", slx))])
                ev_ln1 = emit_ln(p, xf[:, slx, :], xn[:, 0, :], st, sq, 32 + s, ev_xl, xn_f32=qf3[:, :],
                                 war_dep=ev_stage_free.get(("xn", 0)))
                ev_stage_free[("ln", slx)] = ev_ln1
                ev_ln2 = emit_ln(p, qf3[:, :], xn[:, 1, :], st, sq, 48 + s, ev_ln1,
                                 g_ap=gr[:, :], b_ap=br[:, :],
                                 war_dep=ev_stage_free.get(("xn", 1)), vscr=qf2[:, :])
                ss = s % 2
                evv, evt = th.transpose_tile(xn[:, 1, :], stg[:, ss, :, :], 0, dep_src=ev_ln2,
                                             evac_deps=[ev_stage_free.get(("stg", ss))])
                ev_stage_free[("xn", 1)] = evt
                kk = s % 2
                def mmq(te, ss=ss, kk=kk):
                    for g in range(8):
                        last = nc.tensor.matmul(psP[kk][:, 0:256], stg[:, ss, g, :], wq_s[:, g, :],
                                                start=(g == 0), stop=(g == 7))
                    return last
                deps = [evv, ev_wg, ev_ppfree.get(kk)]
                ev_mq = p.step("tensor", mmq, deps=deps)
                ev_stage_free[("stg", ss)] = ev_mq
                ev_stg_last[ss] = ev_mq

                def vq1(v, kk=kk):
                    v.tensor_copy(qf[:, 0:256], psP[kk][:, 0:256])
                    v.drain()
                    q3 = qf[:, 0:256].rearrange("p (h d) -> p h d", d=64)
                    v.tensor_tensor(sq[:, 0:256].rearrange("p (h d) -> p h d", d=64), q3, q3, ALU.mult)
                    v.drain()
                    return v.tensor_reduce(nrm[:, 0:4], sq[:, 0:256].rearrange("p (h d) -> p h d", d=64),
                                           axis=AX.X, op=ALU.add)
                ev_v1 = p.step("vector", vq1, deps=[ev_mq])
                ev_s1 = p.step("scalar", lambda s2: s2.activation(nrm[:, 0:4], nrm[:, 0:4], AF.Sqrt),
                               deps=[ev_v1])
                def vq2(v, kk=kk):
                    v.tensor_scalar(nrm[:, 0:4], nrm[:, 0:4], 1e-12, None, ALU.max)
                    v.drain()
                    v.reciprocal(nrm[:, 0:4], nrm[:, 0:4])
                    v.drain()
                    q3 = qf[:, 0:256].rearrange("p (h d) -> p h d", d=64)
                    v.tensor_tensor(qf2[:, 0:256].rearrange("p (h d) -> p h d", d=64), q3,
                                    nrm[:, 0:4].unsqueeze(2).broadcast_to([128, 4, 64]), ALU.mult)
                    v.drain()
                    return v.tensor_tensor(qb[:, 1, 0:256].rearrange("p (h d) -> p h d", d=64),
                                           qf2[:, 0:256].rearrange("p (h d) -> p h d", d=64),
                                           qsr[:, :].unsqueeze(1).broadcast_to([128, 4, 64]), ALU.mult)
                ev_v2 = p.step("vector", vq2, deps=[ev_s1, ev_qT_done[-1] if ev_qT_done else None])
                ev_ppfree[kk] = ev_v2

                xk = p.xkey(1)
                def tq(e, s=s, xk=xk):
                    for b2 in range(2):
                        e.dma_start(out=qT[:, b2, s * 128:(s + 1) * 128],
                                    in_=qb[:, 1, b2 * 128:(b2 + 1) * 128],
                                    transpose=True).then_inc(p.sem(xk), 16)
                ev_eq = p.step("sync", tq, deps=[ev_v2], inc=32, self_inc=True, key=xk)
                ev_qT_done.append(ev_eq)

            ev_ptfree = {}
            ev_psSfree = {}
            ev_psAfree = {}
            last_vec_evac = None
            for gl in range(2):
                for e in range(2):
                    hl = 2 * gl + e
                    h = 4 * hg + hl
                    for s in range(NT):
                        chunks = list(range(2 * s + 2)) + [32]
                        nch = len(chunks)
                        ngrp = (nch + 3) // 4
                        av = s % 2
                        first_av = True
                        for gi in range(ngrp):
                            c0 = gi * 4
                            ncur = min(4, nch - c0)
                            bk = gi % 2
                            ptk = gi % 2
                            def mmS(te, gl=gl, e=e, s=s, c0=c0, ncur=ncur, bk=bk, h=h, chunks=chunks):
                                for ci in range(ncur):
                                    j = chunks[c0 + ci]
                                    nc.tensor.matmul(
                                        psS[bk][:, ci * 128:(ci + 1) * 128],
                                        jcr[:, j * 128:(j + 1) * 128],
                                        sps[:, h * 128:(h + 1) * 128],
                                        start=True, stop=False)
                                    nc.tensor.matmul(
                                        psS[bk][:, ci * 128:(ci + 1) * 128],
                                        sps[:, h * 128:(h + 1) * 128],
                                        iqr[:, s * 128:(s + 1) * 128],
                                        start=False, stop=False)
                                    last = nc.tensor.matmul(
                                        psS[bk][:, ci * 128:(ci + 1) * 128],
                                        kT[64 * e:64 * e + 64, gl, j * 128:(j + 1) * 128],
                                        qT[64 * e:64 * e + 64, gl, s * 128:(s + 1) * 128],
                                        start=False, stop=True)
                                return last
                            deps = [ev_kT_done, ev_vs_done, ev_qT_done[s], ev_psSfree.get(bk)]
                            ev_mS = p.step("tensor", mmS, deps=deps)

                            ev_m = ev_mS
                            if any(2 * s <= chunks[c0 + ci] <= 2 * s + 1 for ci in range(ncur)):
                                def vmask(v, s=s, c0=c0, ncur=ncur, bk=bk, chunks=chunks):
                                    last = None
                                    for ci in range(ncur):
                                        j = chunks[c0 + ci]
                                        if 2 * s <= j <= 2 * s + 1:
                                            last = v.scalar_tensor_tensor(
                                                psS[bk][:, ci * 128:(ci + 1) * 128],
                                                msk[:, j, :], 3.0e38,
                                                psS[bk][:, ci * 128:(ci + 1) * 128],
                                                ALU.mult, ALU.add)
                                    return last
                                ev_m = p.step("vector", vmask, deps=[ev_mS, ev_wl])

                            def sexp(sc, ncur=ncur, bk=bk, ptk=ptk):
                                return sc.activation(pt[:, ptk, 0:ncur * 128], psS[bk][:, 0:ncur * 128],
                                                     AF.Exp, bias=m0b[:, :])
                            ev_exp = p.step("scalar", sexp, deps=[ev_m, ev_ptfree.get(ptk), ev_init])
                            ev_psSfree[bk] = ev_exp

                            def mmA(te, s=s, c0=c0, ncur=ncur, ptk=ptk, av=av,
                                    first=first_av, h=h, nch=nch, chunks=chunks):
                                for ci in range(ncur):
                                    j = chunks[c0 + ci]
                                    last = nc.tensor.matmul(
                                        psA[av][:, 0:65],
                                        pt[:, ptk, ci * 128:(ci + 1) * 128],
                                        vs[:, j, 65 * (h % 4):65 * (h % 4) + 65],
                                        start=(first and ci == 0), stop=(c0 + ci == nch - 1))
                                return last
                            ev_mA = p.step("tensor", mmA,
                                           deps=[ev_exp] + ([ev_psAfree.get(av)] if first_av else []))
                            ev_ptfree[ptk] = ev_mA
                            first_av = False
                        def vev(v, h=h, s=s, av=av):
                            v.tensor_scalar(rsb[:, 0:1], psA[av][:, 64:65], 1e-30, None, ALU.add)
                            v.drain()
                            v.reciprocal(rsb[:, 0:1], rsb[:, 0:1])
                            v.drain()
                            return v.tensor_scalar(ao[:, s, 64 * h:64 * h + 64],
                                                   psA[av][:, 0:64], rsb[:, 0:1], None, ALU.mult)
                        ev_psAfree[av] = p.step("vector", vev, deps=[ev_mA])
                        last_vec_evac = ev_psAfree[av]
            ev_attn_done_prev = last_vec_evac

        ev_out = []
        ev_stg_free = dict(ev_stg_last)
        for s in range(NT):
            ss = s % 2
            evv, evt = th.transpose_tile(ao[:, s, :], stg[:, ss, :, :], 0,
                                         dep_src=ev_attn_done_prev,
                                         evac_deps=[ev_stg_free.get(ss)])
            ev_xr = dma_step(p, "gpsimd", [(xf[:, s % 4, :], xq[s * 128:(s + 1) * 128, :])],
                             deps=[ev_ydma[s - 4] if s >= 4 else ev_attn_done_prev])
            for oc in range(2):
                kk = oc
                def mmo(te, s=s, oc=oc, kk=kk, ss=ss):
                    for g in range(8):
                        last = nc.tensor.matmul(psP[kk][:, :], stg[:, ss, g, :],
                                                wo_s[:, g, oc * 512:(oc + 1) * 512],
                                                start=(g == 0), stop=(g == 7))
                    return last
                deps = [evv, ev_wl, ev_out[-2] if len(ev_out) >= 2 else None]
                ev_mo = p.step("tensor", mmo, deps=deps)
                ev_stg_free[ss] = ev_mo
                def veo(v, s=s, oc=oc, kk=kk):
                    return v.tensor_tensor(xf[:, s % 4, oc * 512:(oc + 1) * 512], psP[kk][:, :],
                                           xf[:, s % 4, oc * 512:(oc + 1) * 512], ALU.add)
                ev_out.append(p.step("vector", veo, deps=[ev_mo, ev_xr]))
            ev_ydma.append(dma_step(p, "sync", [(y[s * 128:(s + 1) * 128, :], xf[:, s % 4, :])],
                           deps=[ev_out[-2], ev_out[-1]], exact=False))
        p.final_wait("sync")
        p.emit()
    return nc


def run_l0attn(x_full, norm_g, norm_b, Wq, Wkv, q_scale, k_scale, null_kv, Wout):
    import math
    nc = _cached("l0attn", build_l0attn)
    f32 = np.float32
    x_full = np.asarray(x_full, f32)
    start = 2.0 ** (-2.0 ** (-(math.log2(16) - 3)))
    slopes = np.array([start * start ** i for i in range(16)], f32)
    nk = np.asarray(null_kv, f32).reshape(16, 2, 2, 64)
    nullk_tm = np.ascontiguousarray(nk[:, :, 0, :].transpose(1, 0, 2).reshape(2, 1024))
    nullv_tm = np.ascontiguousarray(nk[:, :, 1, :].transpose(1, 0, 2).reshape(2, 1024))
    jcv = np.full(4224, -1e7, f32)
    jcv[:4096] = np.arange(4096) + 2.0
    jcv[4096] = 0.0; jcv[4097] = 1.0
    spos_v = np.repeat(slopes, 128).astype(f32).reshape(1, 16 * 128)
    in_maps = []
    for c in range(NCORES):
        b, pr = c // 2, c % 2
        G = [2 * s + pr for s in range(NT)]
        xb = x_full[b]
        xq_rows = np.ascontiguousarray(np.concatenate([xb[g * 128:(g + 1) * 128] for g in G], 0))
        pos_q = np.concatenate([np.arange(g * 128, (g + 1) * 128) for g in G]).astype(f32)
        iq2 = (-(pos_q + 2.0)).reshape(1, TPC)
        masks = np.zeros((16, 2, 128, 128), f32)
        pp = np.arange(128)[:, None]; cc = np.arange(128)[None, :]
        for s in range(NT):
            for e2 in range(2):
                allowed = (2 * s + e2) * 128 + pp <= (2 * s + pr) * 128 + cc
                masks[s, e2] = np.where(allowed, 0.0, -1.0).astype(f32)
        in_maps.append({
            "xkv": np.ascontiguousarray(xb), "xq": xq_rows,
            "wqw": np.ascontiguousarray(np.asarray(Wq, f32)),
            "wkv": np.ascontiguousarray(np.asarray(Wkv, f32)),
            "wout": np.ascontiguousarray(np.asarray(Wout, f32)),
            "normg": np.ascontiguousarray(np.broadcast_to(np.asarray(norm_g, f32), (128, D))),
            "normb": np.ascontiguousarray(np.broadcast_to(np.asarray(norm_b, f32), (128, D))),
            "qsr": np.ascontiguousarray(np.broadcast_to(np.asarray(q_scale, f32) * 8.0, (128, 64))),
            "ksr": np.ascontiguousarray(np.broadcast_to(np.asarray(k_scale, f32), (128, 64))),
            "nullk": nullk_tm, "nullv": nullv_tm,
            "jcrow": jcv.reshape(1, 4224), "iq2row": iq2,
            "spos": spos_v, "maskt": masks, "ident": IDENT128,
        })
    res = run_bass_kernel_spmd(nc, in_maps, core_ids=list(range(NCORES)))
    out = np.zeros((4, SEQ, D), f32)
    for c in range(NCORES):
        b, pr = c // 2, c % 2
        yv = res.results[c]["y"]
        for s in range(NT):
            g = 2 * s + pr
            out[b, g * 128:(g + 1) * 128] = yv[s * 128:(s + 1) * 128]
    return out



def build_lnT():
    nc = bass.Bass("TRN2", target_bir_lowering=False, debug=False)
    x = nc.dram_tensor("x", [TPC, D], F32, kind="ExternalInput").ap()
    ident = nc.dram_tensor("ident", [128, 128], BF16, kind="ExternalInput").ap()
    yT = nc.dram_tensor("yT", [D, TPC], BF16, kind="ExternalOutput").ap()
    with ExitStack() as es:
        sb = lambda name, shape, dt=F32: es.enter_context(nc.sbuf_tensor(name, shape, dt))
        xf = sb("xf", [128, 4, D]); sq = sb("sq", [128, D]); st = sb("st", [128, 4 * NT])
        xn = sb("xn", [128, 2, D], BF16)
        xnT = sb("xnT", [128, 8, TPC], BF16)
        idb = sb("idb", [128, 128], BF16)
        psT = es.enter_context(nc.psum_tensor("psT", [128, 1024], BF16))
        p = es.enter_context(Prog(nc))
        ev_i = dma_step(p, "gpsimd", [(idb[:], ident)])
        th = TransposeHelper(p, nc, psT, idb)
        th.prev_evac = ev_i
        ev_ld = {}
        ev_tp = {}
        for t in range(NT):
            slx = t % 4
            ev_xl = dma_step(p, "gpsimd", [(xf[:, slx, :], x[t * 128:(t + 1) * 128, :])],
                             deps=[ev_ld.get(slx)])
            sl = t % 2
            ev_ln = emit_ln(p, xf[:, slx, :], xn[:, sl, :], st, sq, t, ev_xl,
                            war_dep=ev_tp.get(sl))
            ev_ld[slx] = ev_ln
            evv, evt = th.transpose_tile(xn[:, sl, :], xnT, t * 128, dep_src=ev_ln)
            ev_tp[sl] = evt
        dma_step(p, "sync", [(yT.rearrange("(g p) c -> p g c", p=128), xnT[:, :, :])],
                 deps=[th.prev_evac], exact=False)
        p.final_wait("sync")
        p.emit()
    return nc


def run_lnT(x_full):
    nc = _cached("lnT", build_lnT)
    shards = split_tokens(x_full)
    in_maps = [{"x": s, "ident": IDENT128} for s in shards]
    res = run_bass_kernel_spmd(nc, in_maps, core_ids=list(range(NCORES)))
    return np.concatenate([r["yT"] for r in res.results], axis=1)



def build_l2b():
    nc = bass.Bass("TRN2", target_bir_lowering=False, debug=False)
    xnT_h = nc.dram_tensor("xnT", [D, TOKS], BF16, kind="ExternalInput").ap()
    w_in = nc.dram_tensor("w_in", [D, 512], F32, kind="ExternalInput").ap()
    convw = nc.dram_tensor("convw", [256, 4], F32, kind="ExternalInput").ap()
    convb = nc.dram_tensor("convb", [256, 1], F32, kind="ExternalInput").ap()
    alog = nc.dram_tensor("alog", [256, 16], F32, kind="ExternalInput").ap()
    dvec = nc.dram_tensor("dvec", [256, 1], F32, kind="ExternalInput").ap()
    yT_h = nc.dram_tensor("yT", [256, TOKS], BF16, kind="ExternalOutput").ap()
    xs_scr = nc.dram_tensor("xs_scr", [512, TOKS], BF16, kind="Internal").ap()
    NB = 4
    with ExitStack() as es:
        sb = lambda name, shape, dt=F32: es.enter_context(nc.sbuf_tensor(name, shape, dt))
        pst_ = lambda name, shape, dt=F32: es.enter_context(nc.psum_tensor(name, shape, dt))
        stg = sb("stg", [128, 2, 8, 512], BF16)
        wi = sb("wi", [128, 8, 512], BF16)
        ev_st = sb("ev_st", [128, 4, 512], BF16)
        cw = sb("cw", [128, 2, 4]); cb = sb("cb", [128, 2]); dv = sb("dv", [128, 2])
        aa = sb("aa", [128, 2, 16])
        xsb = sb("xsb", [128, NB, 4104], BF16)
        srb = sb("srb", [128, NB, 4096], BF16)
        xcv = sb("xcv", [128, 4096])
        dlt = sb("dlt", [128, 4096])
        bx2 = sb("bx2", [128, 4096])
        at = sb("at", [128, 4096])
        ht = sb("ht", [128, 4096])
        yac = sb("yac", [128, 4096])
        ytb = sb("ytb", [128, 2, 4096], BF16)
        psB = [pst_(f"psB{i}", [128, 512]) for i in range(4)]
        p = es.enter_context(Prog(nc))

        ev_w = dma_step(p, "gpsimd", [
            *[(wi[:, g, :], w_in[g * 128:(g + 1) * 128, :]) for g in range(8)],
            (cw[:, 0, :], convw[0:128, :]), (cw[:, 1, :], convw[128:256, :]),
            (cb[:, 0:1], convb[0:128, :]), (cb[:, 1:2], convb[128:256, :]),
            (aa[:, 0, :], alog[0:128, :]), (aa[:, 1, :], alog[128:256, :]),
            (dv[:, 0:1], dvec[0:128, :]), (dv[:, 1:2], dvec[128:256, :]),
        ])
        ev_sa = p.step("scalar", lambda s: s.activation(aa.rearrange("p a b -> p (a b)"),
                                                        aa.rearrange("p a b -> p (a b)"), AF.Exp),
                       deps=[ev_w])
        ev_va = p.step("vector", lambda v: v.tensor_scalar(aa.rearrange("p a b -> p (a b)"),
                                                           aa.rearrange("p a b -> p (a b)"),
                                                           -1.0, None, ALU.mult), deps=[ev_sa])

        ev_stg = [None, None]
        ev_ev = {}
        ev_mm_last = {}
        for tc in range(TOKS // 512):
            sl = tc % 2
            ev_ld = dma_step(p, "gpsimd",
                             [(stg[:, sl, :, :], xnT_h[:, tc * 512:(tc + 1) * 512]
                               .rearrange("(g p) c -> p g c", p=128))],
                             deps=[ev_stg[sl]])
            for ob in range(4):
                def mmp(te, sl=sl, ob=ob):
                    for g in range(8):
                        last = nc.tensor.matmul(psB[ob][:, :], wi[:, g, ob * 128:(ob + 1) * 128],
                                                stg[:, sl, g, :], start=(g == 0), stop=(g == 7))
                    return last
                ev_mm = p.step("tensor", mmp, deps=[ev_ld, ev_w, ev_ev.get(ob)])
                ev_mm_last[(tc, ob)] = ev_mm
                if ob < 2:
                    def evac(v, ob=ob, tc=tc):
                        return v.tensor_copy(ev_st[:, ob, :], psB[ob][:, :])
                    ev_ev[ob] = p.step("vector", evac, deps=[ev_mm])
                else:
                    def evac(s, ob=ob, tc=tc):
                        return s.activation(ev_st[:, ob, :], psB[ob][:, :], AF.Silu)
                    ev_ev[ob] = p.step("scalar", evac, deps=[ev_mm])
                rowbase = (ob % 2) * 128 if ob < 2 else 256 + (ob - 2) * 128
                ev_out = dma_step(p, "sync",
                                  [(xs_scr[rowbase:rowbase + 128, tc * 512:(tc + 1) * 512],
                                    ev_st[:, ob, :])], deps=[ev_ev[ob]])
                ev_ev[ob] = ev_out
            ev_stg[sl] = ev_mm_last[(tc, 3)]

        ev_scr_list = [ev_ev[ob] for ob in range(4)]
        ev_ydma = {}

        for blk in range(2):
            ev_l1 = dma_step(p, "gpsimd", [
                *[(xsb[:, b, 4:4100], xs_scr[blk * 128:blk * 128 + 128, b * 4096:(b + 1) * 4096])
                  for b in range(NB)],
                *[(srb[:, b, :], xs_scr[256 + blk * 128:256 + blk * 128 + 128, b * 4096:(b + 1) * 4096])
                  for b in range(NB)],
            ], deps=ev_scr_list if blk == 0 else [ev_ylast])
            ev_pad = p.step("vector", lambda v: (v.memset(xsb[:, :, 0:4], 0.0),
                                                 v.memset(xsb[:, :, 4100:4104], 0.0))[-1],
                            deps=[ev_l1])
            for b in range(NB):
                def vconv(v, b=b, blk=blk):
                    v.tensor_scalar(xcv[:, :], xsb[:, b, 1:4097], cw[:, blk, 0:1], None, ALU.mult)
                    for k in range(1, 3):
                        v.scalar_tensor_tensor(xcv[:, :], xsb[:, b, 1 + k:4097 + k],
                                               cw[:, blk, k:k + 1], xcv[:, :], ALU.mult, ALU.add)
                    return v.scalar_tensor_tensor(xcv[:, :], xsb[:, b, 4:4100],
                                                  cw[:, blk, 3:4], xcv[:, :], ALU.mult, ALU.add)
                ev_cv = p.step("vector", vconv, deps=[ev_pad, ev_va])
                ev_sl = p.step("scalar",
                               (lambda blk: lambda s: s.activation(xcv[:, :], xcv[:, :], AF.Silu,
                                                                   bias=cb[:, blk:blk + 1]))(blk),
                               deps=[ev_cv])
                def vsp1(v):
                    v.tensor_scalar(dlt[:, :], xcv[:, :], -1.0, None, ALU.mult)
                    v.tensor_tensor(dlt[:, :], dlt[:, :], xcv[:, :], ALU.max)
                    v.tensor_scalar(dlt[:, :], dlt[:, :], -1.0, None, ALU.mult)
                    return v.tensor_tensor(bx2[:, :], xcv[:, :], xcv[:, :], ALU.mult)
                ev_v1 = p.step("vector", vsp1, deps=[ev_sl])
                def ssp(s):
                    s.activation(dlt[:, :], dlt[:, :], AF.Exp)
                    return s.activation(dlt[:, :], dlt[:, :], AF.Ln, bias=1.0)
                ev_s1 = p.step("scalar", ssp, deps=[ev_v1])
                def vsp2(v):
                    v.tensor_scalar(at[:, :], xcv[:, :], 0.0, None, ALU.max)
                    return v.tensor_tensor(dlt[:, :], dlt[:, :], at[:, :], ALU.add)
                ev_v2 = p.step("vector", vsp2, deps=[ev_s1])
                ev_h = None
                ev_y0 = None
                for n in range(16):
                    def sexp(s, n=n, blk=blk):
                        return s.activation(at[:, :], dlt[:, :], AF.Exp,
                                            scale=aa[:, blk, n:n + 1])
                    ev_e = p.step("scalar", sexp, deps=[ev_v2, ev_h])
                    def vscan(v, n=n):
                        v.tensor_tensor_scan(ht[:, :], at[:, :], bx2[:, :], 0.0, ALU.mult, ALU.add)
                        if n == 0:
                            return v.tensor_copy(yac[:, :], ht[:, :])
                        return v.tensor_tensor(yac[:, :], yac[:, :], ht[:, :], ALU.add)
                    ev_h = p.step("vector", vscan, deps=[ev_e])
                ysl = b % 2
                def vgate(v, b=b, blk=blk, ysl=ysl):
                    v.tensor_tensor(yac[:, :], yac[:, :], xcv[:, :], ALU.mult)
                    v.tensor_tensor(yac[:, :], yac[:, :], srb[:, b, :], ALU.mult)
                    return v.scalar_tensor_tensor(ytb[:, ysl, :], xcv[:, :], dv[:, blk:blk + 1],
                                                  yac[:, :], ALU.mult, ALU.add)
                ev_g = p.step("vector", vgate, deps=[ev_h, ev_ydma.get(ysl)])
                ev_ydma[ysl] = dma_step(p, "sync",
                                        [(yT_h[blk * 128:(blk + 1) * 128, b * 4096:(b + 1) * 4096],
                                          ytb[:, ysl, :])], deps=[ev_g], exact=False)
            ev_ylast = ev_ydma[(NB - 1) % 2]
        p.final_wait("sync")
        p.emit()
    return nc


def run_l2b(xnT, W_in, conv_w, conv_b, A_log, Dv):
    nc = _cached("l2b", build_l2b)
    f32 = np.float32
    W_inT = np.asarray(W_in, f32).T
    in_maps = []
    for c in range(NCORES):
        sl = slice(256 * c, 256 * (c + 1))
        wcols = np.concatenate([W_inT[:, sl], W_inT[:, 2048 + 256 * c: 2048 + 256 * (c + 1)]], 1)
        in_maps.append({
            "xnT": xnT,
            "w_in": np.ascontiguousarray(wcols),
            "convw": np.ascontiguousarray(np.asarray(conv_w, f32)[sl, 0, :]),
            "convb": np.ascontiguousarray(np.asarray(conv_b, f32)[sl].reshape(256, 1)),
            "alog": np.ascontiguousarray(np.asarray(A_log, f32)[sl]),
            "dvec": np.ascontiguousarray(np.asarray(Dv, f32)[sl].reshape(256, 1)),
        })
    res = run_bass_kernel_spmd(nc, in_maps, core_ids=list(range(NCORES)))
    return np.concatenate([r["yT"] for r in res.results], axis=0)



def build_l2c():
    nc = bass.Bass("TRN2", target_bir_lowering=False, debug=False)
    x = nc.dram_tensor("x", [TPC, D], F32, kind="ExternalInput").ap()
    yT_h = nc.dram_tensor("yT", [2048, TPC], BF16, kind="ExternalInput").ap()
    woT = nc.dram_tensor("woT", [2048, D], F32, kind="ExternalInput").ap()
    y = nc.dram_tensor("y", [TPC, D], F32, kind="ExternalOutput").ap()
    with ExitStack() as es:
        sb = lambda name, shape, dt=F32: es.enter_context(nc.sbuf_tensor(name, shape, dt))
        pst_ = lambda name, shape, dt=F32: es.enter_context(nc.psum_tensor(name, shape, dt))
        xf = sb("xf", [128, NT, D])
        yts = sb("yts", [128, 16, TPC], BF16)
        wos = sb("wos", [128, 16, D], BF16)
        ps = [pst_(f"ps{i}", [128, 512]) for i in range(2)]
        p = es.enter_context(Prog(nc))
        ev_x = [dma_step(p, "gpsimd", [(xf[:, t, :], x[t * 128:(t + 1) * 128, :])]) for t in range(NT)]
        ev_w = dma_step(p, "gpsimd", [(wos[:, :, :], woT.rearrange("(a p) c -> p a c", p=128))])
        ev_y = dma_step(p, "gpsimd", [(yts[:, :, :], yT_h.rearrange("(a p) c -> p a c", p=128))])
        ev_out = []
        for t in range(NT):
            for oc in range(2):
                kk = oc
                def mmo(te, t=t, oc=oc, kk=kk):
                    for ib in range(16):
                        last = nc.tensor.matmul(ps[kk][:, :], yts[:, ib, t * 128:(t + 1) * 128],
                                                wos[:, ib, oc * 512:(oc + 1) * 512],
                                                start=(ib == 0), stop=(ib == 15))
                    return last
                deps = [ev_w, ev_y, ev_out[-2] if len(ev_out) >= 2 else None]
                ev_mo = p.step("tensor", mmo, deps=deps)
                def veo(v, t=t, oc=oc, kk=kk):
                    return v.tensor_tensor(xf[:, t, oc * 512:(oc + 1) * 512], ps[kk][:, :],
                                           xf[:, t, oc * 512:(oc + 1) * 512], ALU.add)
                ev_out.append(p.step("vector", veo, deps=[ev_mo, ev_x[t]]))
            dma_step(p, "sync", [(y[t * 128:(t + 1) * 128, :], xf[:, t, :])],
                     deps=[ev_out[-2], ev_out[-1]], exact=False)
        p.final_wait("sync")
        p.emit()
    return nc


def run_l2c(x_full, yT, W_out):
    nc = _cached("l2c", build_l2c)
    f32 = np.float32
    shards = split_tokens(x_full)
    woT = np.ascontiguousarray(np.asarray(W_out, f32).T)
    in_maps = [{"x": shards[c], "yT": np.ascontiguousarray(yT[:, c * TPC:(c + 1) * TPC]),
                "woT": woT} for c in range(NCORES)]
    res = run_bass_kernel_spmd(nc, in_maps, core_ids=list(range(NCORES)))
    return join_tokens([r["y"] for r in res.results])


def run_l2attn(x_full, W_in, conv_w, conv_b, A_log, Dv, W_out):
    xnT = run_lnT(x_full)
    yT = run_l2b(xnT, W_in, conv_w, conv_b, A_log, Dv)
    return run_l2c(x_full, yT, W_out)



def build_l1b():
    nc = bass.Bass("TRN2", target_bir_lowering=False, debug=False)
    rrows = nc.dram_tensor("rrows", [128, TOKS], BF16, kind="ExternalInput").ap()
    kidft = nc.dram_tensor("kidft", [2, 32, 128, SEQ], BF16, kind="ExternalInput").ap()
    fwdt = nc.dram_tensor("fwdt", [2, 33, SEQ, 128], BF16, kind="ExternalInput").ap()
    invt = nc.dram_tensor("invt", [2, 32, 4224, 128], BF16, kind="ExternalInput").ap()
    g0im = nc.dram_tensor("g0im", [128, 32], F32, kind="ExternalInput").ap()
    w2re = nc.dram_tensor("w2re", [128, 32], F32, kind="ExternalInput").ap()
    w2im = nc.dram_tensor("w2im", [128, 32], F32, kind="ExternalInput").ap()
    dre_b = nc.dram_tensor("dre_b", [128, 64], F32, kind="ExternalInput").ap()
    dim_b = nc.dram_tensor("dim_b", [128, 64], F32, kind="ExternalInput").ap()
    cre = nc.dram_tensor("cre", [128, 64], F32, kind="ExternalInput").ap()
    cim = nc.dram_tensor("cim", [128, 64], F32, kind="ExternalInput").ap()
    bre_b = nc.dram_tensor("bre_b", [128, 64], F32, kind="ExternalInput").ap()
    bim_b = nc.dram_tensor("bim_b", [128, 64], F32, kind="ExternalInput").ap()
    pre_b = nc.dram_tensor("pre_b", [128, 64], F32, kind="ExternalInput").ap()
    pim_b = nc.dram_tensor("pim_b", [128, 64], F32, kind="ExternalInput").ap()
    qcols = nc.dram_tensor("qcols", [64, 6], F32, kind="ExternalInput").ap()
    stepv = nc.dram_tensor("stepv", [1, 1], F32, kind="ExternalInput").ap()
    dval = nc.dram_tensor("dval", [1, 1], F32, kind="ExternalInput").ap()
    onesf = nc.dram_tensor("onesf", [1, 128], F32, kind="ExternalInput").ap()
    ident = nc.dram_tensor("ident", [128, 128], BF16, kind="ExternalInput").ap()
    yT_h = nc.dram_tensor("yT", [128, TOKS], BF16, kind="ExternalOutput").ap()
    NKC = 33

    with ExitStack() as es:
        sb = lambda name, shape, dt=F32: es.enter_context(nc.sbuf_tensor(name, shape, dt))
        pst_ = lambda name, shape, dt=F32: es.enter_context(nc.psum_tensor(name, shape, dt))
        rT = sb("rT", [128, 32, 512], BF16)
        atb = sb("atb", [128, 2, 32, 128], BF16)
        cdf = sb("cdf", [128, 6, 64])
        cdb = sb("cdb", [128, 2, 128], BF16)
        cdT = sb("cdT", [128, 3, 128], BF16)
        cbb = sb("cbb", [128, 8, 64])
        cbT = sb("cbT", [128, 4, 128], BF16)
        qbs = sb("qbs", [64, 4], BF16)
        smallf = sb("smallf", [128, 64])
        kk0 = sb("kk0", [128, 4, 128])
        k1011 = sb("k1011", [128, 16])
        tabA = sb("tabA", [128, 1, 2, 2048], BF16)
        tabB = sb("tabB", [128, 2, 2, 33, 128], BF16)
        KTc = sb("KTc", [128, 4096], BF16)
        Ktb = sb("Ktb", [128, 32, 128], BF16)
        Kfs = sb("Kfs", [128, 33, 2, 128], BF16)
        ufP = sb("ufP", [128, 33, 2, 512], BF16)
        scr1 = sb("scr1", [128, 512]); scr2 = sb("scr2", [128, 512])
        sc3 = sb("sc3", [128, 512], BF16)
        yst = sb("yst", [128, 2, 128], BF16)
        rsp = sb("rsp", [128, 2])
        idb = sb("idb", [128, 128], BF16)
        ones_s = sb("ones_s", [1, 128])
        g0s = sb("g0s", [128, 4, 32])
        dreb = sb("dreb", [128, 64]); dimb = sb("dimb", [128, 64])
        ps = [pst_(f"ps{i}", [128, 512]) for i in range(4)]
        psK = [pst_(f"psK{i}", [128, 512]) for i in range(2)]
        psT = pst_("psT", [128, 1024], BF16)
        psB2 = pst_("psB2", [128, 8])
        p = es.enter_context(Prog(nc))

        ev_w = dma_step(p, "gpsimd", [
            (g0s[:, 0, :], g0im), (g0s[:, 2, :], w2re), (g0s[:, 3, :], w2im),
            (dreb[:, :], dre_b), (dimb[:, :], dim_b),
            (cbb[:, 0, :], cre), (cbb[:, 1, :], cim),
            (cbb[:, 2, :], bre_b), (cbb[:, 3, :], bim_b),
            (cbb[:, 4, :], pre_b), (cbb[:, 5, :], pim_b),
            (smallf[0:64, 2:8], qcols), (ones_s[:], onesf), (idb[:], ident),
        ])
        def vqb(v):
            c = lambda i: smallf[0:64, i:i + 1]
            v.tensor_tensor(c(8), c(2), c(4), ALU.mult)
            v.tensor_tensor(c(9), c(3), c(5), ALU.mult)
            v.tensor_tensor(c(10), c(2), c(5), ALU.mult)
            v.tensor_tensor(c(11), c(3), c(4), ALU.mult)
            v.tensor_tensor(c(12), c(2), c(6), ALU.mult)
            v.tensor_tensor(c(13), c(3), c(7), ALU.mult)
            v.tensor_tensor(c(14), c(2), c(7), ALU.mult)
            v.tensor_tensor(c(15), c(3), c(6), ALU.mult)
            v.drain()
            v.tensor_tensor(qbs[0:64, 0:1], c(8), c(9), ALU.subtract)
            v.tensor_tensor(qbs[0:64, 1:2], c(10), c(11), ALU.add)
            v.tensor_tensor(qbs[0:64, 2:3], c(12), c(13), ALU.subtract)
            return v.tensor_tensor(qbs[0:64, 3:4], c(14), c(15), ALU.add)
        ev_qb = p.step("vector", vqb, deps=[ev_w])
        ev_step = dma_step(p, "gpsimd", [(smallf[0:1, 0:1], stepv), (smallf[0:1, 1:2], dval)])
        ev_pad0 = p.step("vector", lambda v: v.memset(cdb.rearrange("p a b -> p (a b)"), 0.0))

        def vstep0(v):
            v.tensor_scalar(smallf[0:1, 0:1], smallf[0:1, 0:1], 1e-6, None, ALU.max)
            v.drain()
            return v.reciprocal(smallf[0:1, 0:1], smallf[0:1, 0:1])
        ev_v0 = p.step("vector", vstep0, deps=[ev_step])
        ev_mb = p.step("tensor", lambda te: nc.tensor.matmul(
            psB2[:, 0:2], ones_s[:, 0:128], smallf[0:1, 0:2], start=True, stop=True),
            deps=[ev_v0, ev_w])
        def vstep1(v):
            v.tensor_copy(rsp[:, :], psB2[:, 0:2])
            v.drain()
            return v.tensor_scalar(g0s[:, 1, :], g0s[:, 0, :], rsp[:, 0:1], None, ALU.mult)
        ev_rs = p.step("vector", vstep1, deps=[ev_mb])

        def vcb(v, re_slot, im_slot, bslot):
            v.tensor_tensor(cbb[:, 6, :], cbb[:, 0, :], cbb[:, bslot, :], ALU.mult)
            v.tensor_tensor(cbb[:, 7, :], cbb[:, 1, :], cbb[:, bslot + 1, :], ALU.mult)
            v.tensor_tensor(smallf[:, :], cbb[:, 0, :], cbb[:, bslot + 1, :], ALU.mult)
            v.drain()
            v.tensor_tensor(cdf[:, 0, :], cbb[:, 6, :], cbb[:, 7, :], ALU.subtract)
            v.tensor_tensor(cbb[:, 6, :], cbb[:, 1, :], cbb[:, bslot, :], ALU.mult)
            v.drain()
            v.tensor_tensor(cdf[:, 1, :], smallf[:, :], cbb[:, 6, :], ALU.add)
            v.drain()
            return v.tensor_copy(cdb[:, 0:2, 0:64], cdf[:, 0:2, :])
        ev_cb = p.step("vector", lambda v: vcb(v, 0, 1, 2), deps=[ev_w, ev_pad0])
        def tcb01(te):
            nc.tensor.transpose(psT[:, 0:128], cdb[:, 0, :], idb[:]).then_inc(p.sem("tensor"), 1)
            nc.tensor.transpose(psT[:, 128:256], cdb[:, 1, :], idb[:]).then_inc(p.sem("tensor"), 1)
        ev_t1 = p.step("tensor", tcb01, deps=[ev_cb], inc=2, self_inc=True)
        ev_e1 = p.step("vector", lambda v: v.tensor_copy(
            cbT[:, 0:2, :].rearrange("p a b -> p (a b)"), psT[:, 0:256]), deps=[ev_t1])
        ev_cb2 = p.step("vector", lambda v: vcb(v, 0, 1, 4), deps=[ev_e1])
        ev_t2 = p.step("tensor", tcb01, deps=[ev_cb2], inc=2, self_inc=True)
        ev_e2 = p.step("vector", lambda v: v.tensor_copy(
            cbT[:, 2:4, :].rearrange("p a b -> p (a b)"), psT[:, 0:256]), deps=[ev_t2])

        ev_rt = ev_w
        ev_rl_prev = None
        for b in range(4):
            ev_rl = dma_step(p, "gpsimd", [(KTc[:, :], rrows[:, b * 4096:(b + 1) * 4096])],
                             deps=[ev_rl_prev])
            for j in range(32):
                def trr(te, b=b, j=j):
                    return nc.tensor.transpose(psT[:, 0:128], KTc[:, j * 128:(j + 1) * 128], idb[:])
                ev_tr = p.step("tensor", trr, deps=[ev_rl, ev_rt])
                def err_(v, b=b, j=j):
                    return v.tensor_copy(rT[:, j, b * 128:(b + 1) * 128], psT[:, 0:128])
                ev_rt = p.step("vector", err_, deps=[ev_tr])
            ev_rl_prev = ev_rt

        ev_at = None
        ev_mk_prev = None
        for lc in range(32):
            def vcd(v, lc=lc):
                v.tensor_scalar(cdf[:, 0, :], dreb[:, :], -1.0, None, ALU.mult)
                v.tensor_scalar(cdf[:, 1, :], dimb[:, :], g0s[:, 1, lc:lc + 1], -1.0, ALU.subtract, ALU.mult)
                v.drain()
                v.tensor_tensor(cdf[:, 2, :], cdf[:, 0, :], cdf[:, 0, :], ALU.mult)
                v.tensor_tensor(cdf[:, 3, :], cdf[:, 1, :], cdf[:, 1, :], ALU.mult)
                v.drain()
                v.tensor_tensor(cdf[:, 2, :], cdf[:, 2, :], cdf[:, 3, :], ALU.add)
                v.drain()
                v.reciprocal(cdf[:, 3, :], cdf[:, 2, :])
                v.drain()
                v.tensor_tensor(cdf[:, 4, :], cdf[:, 0, :], cdf[:, 3, :], ALU.mult)
                v.tensor_tensor(cdf[:, 5, :], cdf[:, 1, :], cdf[:, 3, :], ALU.mult)
                v.drain()
                v.tensor_scalar(cdf[:, 5, :], cdf[:, 5, :], -1.0, None, ALU.mult)
                v.drain()
                return v.tensor_copy(cdb[:, 0:2, 0:64], cdf[:, 4:6, :])
            ev_cd = p.step("vector", vcd, deps=[ev_rs, ev_e2, ev_mk_prev])
            def tcd(te, lc=lc):
                nc.tensor.transpose(psT[:, 0:128], cdb[:, 0, :], idb[:]).then_inc(p.sem("tensor"), 1)
                nc.tensor.transpose(psT[:, 128:256], cdb[:, 1, :], idb[:]).then_inc(p.sem("tensor"), 1)
            ev_tc = p.step("tensor", tcd, deps=[ev_cd], inc=2, self_inc=True)
            def ecd(v, lc=lc):
                v.tensor_copy(cdT[:, 0:2, :].rearrange("p a b -> p (a b)"), psT[:, 0:256])
                v.drain()
                return v.tensor_scalar(cdT[0:64, 2, :], cdT[0:64, 1, :], -1.0, None, ALU.mult)
            ev_ec = p.step("vector", ecd, deps=[ev_tc])

            def mmk(te, lc=lc):
                nc.tensor.matmul(psK[0][:, 0:128], cdT[0:64, 0, :], cbT[0:64, 0, :], start=True, stop=False)
                nc.tensor.matmul(psK[0][:, 0:128], cdT[0:64, 2, :], cbT[0:64, 1, :], start=False, stop=True)
                nc.tensor.matmul(psK[0][:, 128:256], cdT[0:64, 0, :], cbT[0:64, 1, :], start=True, stop=False)
                nc.tensor.matmul(psK[0][:, 128:256], cdT[0:64, 1, :], cbT[0:64, 0, :], start=False, stop=True)
                nc.tensor.matmul(psK[1][:, 0:128], cdT[0:64, 0, :], cbT[0:64, 2, :], start=True, stop=False)
                nc.tensor.matmul(psK[1][:, 0:128], cdT[0:64, 2, :], cbT[0:64, 3, :], start=False, stop=True)
                nc.tensor.matmul(psK[1][:, 128:256], cdT[0:64, 0, :], cbT[0:64, 3, :], start=True, stop=False)
                nc.tensor.matmul(psK[1][:, 128:256], cdT[0:64, 1, :], cbT[0:64, 2, :], start=False, stop=True)
                nc.tensor.matmul(psB2[:, 0:4], cdT[0:64, 0, :], qbs[:, :], start=True, stop=True)
                return nc.tensor.matmul(psB2[:, 4:8], cdT[0:64, 1, :], qbs[:, :], start=True, stop=True)
            ev_mk = p.step("tensor", mmk, deps=[ev_ec, ev_at, ev_qb])
            ev_mk_prev = ev_mk

            def vat(v, lc=lc):
                v.tensor_copy(kk0[:, 0:2, :].rearrange("p a b -> p (a b)"),
                              psK[0][:, 0:256])
                v.tensor_copy(kk0[:, 2:4, :].rearrange("p a b -> p (a b)"), psK[1][:, 0:256])
                v.tensor_copy(k1011[:, 0:8], psB2[:, 0:8])
                v.drain()
                c = lambda i: k1011[:, i:i + 1]
                v.tensor_tensor(c(8), c(0), c(5), ALU.subtract)
                v.tensor_tensor(c(9), c(1), c(4), ALU.add)
                v.tensor_tensor(c(10), c(2), c(7), ALU.subtract)
                v.tensor_tensor(c(11), c(3), c(6), ALU.add)
                v.drain()
                v.tensor_scalar(c(10), c(10), 1.0, None, ALU.add)
                v.drain()
                v.tensor_tensor(c(12), c(10), c(10), ALU.mult)
                v.tensor_tensor(c(13), c(11), c(11), ALU.mult)
                v.drain()
                v.tensor_tensor(c(12), c(12), c(13), ALU.add)
                v.drain()
                v.reciprocal(c(12), c(12))
                v.drain()
                v.tensor_tensor(c(13), c(8), c(10), ALU.mult)
                v.tensor_tensor(c(14), c(9), c(11), ALU.mult)
                v.drain()
                v.tensor_tensor(c(13), c(13), c(14), ALU.add)
                v.tensor_tensor(c(14), c(9), c(10), ALU.mult)
                v.tensor_tensor(c(15), c(8), c(11), ALU.mult)
                v.drain()
                v.tensor_tensor(c(14), c(14), c(15), ALU.subtract)
                v.drain()
                v.tensor_tensor(c(13), c(13), c(12), ALU.mult)
                v.tensor_tensor(c(14), c(14), c(12), ALU.mult)
                v.drain()
                v.tensor_scalar(scr1[:, 0:128], kk0[:, 2, :], c(13), None, ALU.mult)
                v.tensor_scalar(scr1[:, 128:256], kk0[:, 3, :], c(14), None, ALU.mult)
                v.tensor_scalar(scr2[:, 0:128], kk0[:, 2, :], c(14), None, ALU.mult)
                v.tensor_scalar(scr2[:, 128:256], kk0[:, 3, :], c(13), None, ALU.mult)
                v.drain()
                v.tensor_tensor(scr1[:, 0:128], scr1[:, 0:128], scr1[:, 128:256], ALU.subtract)
                v.tensor_tensor(scr2[:, 0:128], scr2[:, 0:128], scr2[:, 128:256], ALU.add)
                v.drain()
                v.tensor_tensor(kk0[:, 0, :], kk0[:, 0, :], scr1[:, 0:128], ALU.subtract)
                v.tensor_tensor(kk0[:, 1, :], kk0[:, 1, :], scr2[:, 0:128], ALU.subtract)
                v.drain()
                v.tensor_scalar(scr1[:, 0:128], kk0[:, 0, :], g0s[:, 2, lc:lc + 1], None, ALU.mult)
                v.tensor_scalar(scr1[:, 128:256], kk0[:, 1, :], g0s[:, 3, lc:lc + 1], None, ALU.mult)
                v.tensor_scalar(scr2[:, 0:128], kk0[:, 0, :], g0s[:, 3, lc:lc + 1], None, ALU.mult)
                v.tensor_scalar(scr2[:, 128:256], kk0[:, 1, :], g0s[:, 2, lc:lc + 1], None, ALU.mult)
                v.drain()
                v.tensor_tensor(atb[:, 0, lc, :], scr1[:, 0:128], scr1[:, 128:256], ALU.subtract)
                return v.tensor_tensor(atb[:, 1, lc, :], scr2[:, 0:128], scr2[:, 128:256], ALU.add)
            ev_at = p.step("vector", vat, deps=[ev_mk])

        ev_tab = [None, None]
        ev_kev = {}
        for half in range(2):
            ev_acc = None
            for lc in range(32):
                bf = 0
                ev_tl = dma_step(p, "gpsimd", [
                    (tabA[:, bf, 0, :], kidft[0, lc, :, half * 2048:(half + 1) * 2048]),
                    (tabA[:, bf, 1, :], kidft[1, lc, :, half * 2048:(half + 1) * 2048]),
                ], deps=[ev_tab[bf]])
                def mmK(te, lc=lc, bf=bf, first=(lc == 0)):
                    for tb in range(4):
                        nc.tensor.matmul(ps[tb][:, :], atb[:, 0, lc, :],
                                         tabA[:, bf, 0, tb * 512:(tb + 1) * 512],
                                         start=first, stop=False)
                        last = nc.tensor.matmul(ps[tb][:, :], atb[:, 1, lc, :],
                                                tabA[:, bf, 1, tb * 512:(tb + 1) * 512],
                                                start=False, stop=(lc == 31))
                    return last
                deps = [ev_tl, ev_at]
                if lc == 0:
                    deps += [ev_kev.get(tb) for tb in range(4)]
                ev_acc = p.step("tensor", mmK, deps=deps)
                ev_tab[bf] = ev_acc
            for tb in range(4):
                def ekv(v, half=half, tb=tb):
                    return v.tensor_copy(KTc[:, half * 2048 + tb * 512: half * 2048 + (tb + 1) * 512],
                                         ps[tb][:, :])
                ev_kev[tb] = p.step("vector", ekv, deps=[ev_acc, ev_rt])

        ev_kt = ev_kev[3]
        for j in range(32):
            def tkk(te, j=j):
                return nc.tensor.transpose(psT[:, 0:128], KTc[:, j * 128:(j + 1) * 128], idb[:])
            ev_t = p.step("tensor", tkk, deps=[ev_kev[j % 4], ev_kt])
            def ekk(v, j=j):
                return v.tensor_copy(Ktb[:, j, :], psT[:, 0:128])
            ev_kt = p.step("vector", ekk, deps=[ev_t])

        ev_tabB = [None, None]
        ev_ufev = None
        ev_kfev = None
        for kc in range(NKC):
            bf = kc % 2
            ev_tl = dma_step(p, "gpsimd", [
                (tabB[:, bf, 0, 0:32, :], fwdt[0, kc].rearrange("(g p) c -> p g c", p=128)),
                (tabB[:, bf, 1, 0:32, :], fwdt[1, kc].rearrange("(g p) c -> p g c", p=128)),
            ], deps=[ev_tabB[bf]])
            for comp in range(2):
                def mmF(te, kc=kc, bf=bf, comp=comp):
                    for tc in range(32):
                        last = nc.tensor.matmul(ps[comp][:, :], tabB[:, bf, comp, tc, :],
                                                rT[:, tc, :], start=(tc == 0), stop=(tc == 31))
                    for tc in range(32):
                        last = nc.tensor.matmul(ps[2 + comp][:, 0:128], tabB[:, bf, comp, tc, :],
                                                Ktb[:, tc, :], start=(tc == 0), stop=(tc == 31))
                    return last
                deps = [ev_tl, ev_rt, ev_kt, ev_ufev, ev_kfev]
                ev_mmF = p.step("tensor", mmF, deps=deps)
                ev_tabB[bf] = ev_mmF
                def euf(v, kc=kc, comp=comp):
                    v.tensor_copy(ufP[:, kc, comp, :], ps[comp][:, :])
                    return v.tensor_copy(Kfs[:, kc, comp, :], ps[2 + comp][:, 0:128])
                ev_ufev = p.step("vector", euf, deps=[ev_mmF])
            ev_kfev = ev_ufev

        ev_pp = ev_ufev
        for kc in range(NKC):
            def vprod(v, kc=kc):
                u_re = ufP[:, kc, 0, :].rearrange("p (b c) -> p b c", c=128)
                u_im = ufP[:, kc, 1, :].rearrange("p (b c) -> p b c", c=128)
                kre = Kfs[:, kc, 0, :].unsqueeze(1).broadcast_to([128, 4, 128])
                kim = Kfs[:, kc, 1, :].unsqueeze(1).broadcast_to([128, 4, 128])
                s1 = scr1.rearrange("p (b c) -> p b c", c=128)
                s2 = scr2.rearrange("p (b c) -> p b c", c=128)
                v.tensor_tensor(s1, u_re, kre, ALU.mult)
                v.tensor_tensor(s2, u_im, kim, ALU.mult)
                v.drain()
                v.tensor_tensor(scr1[:, :], scr1[:, :], scr2[:, :], ALU.subtract)
                v.tensor_tensor(s2, u_re, kim, ALU.mult)
                v.drain()
                v.tensor_tensor(ufP[:, kc, 0, :].rearrange("p (b c) -> p b c", c=128), u_im, kre, ALU.mult)
                v.drain()
                v.tensor_tensor(ufP[:, kc, 1, :], ufP[:, kc, 0, :], scr2[:, :], ALU.add)
                v.drain()
                return v.tensor_copy(ufP[:, kc, 0, :], scr1[:, :])
            ev_pp = p.step("vector", vprod, deps=[ev_pp])

        ev_tabB2 = [None, None]
        ev_inv_ps = None
        ev_sg_prev = None
        ev_yst = {}
        ev_ydma = {}
        for tc in range(32):
            bf = tc % 2
            ev_tl = dma_step(p, "gpsimd", [
                (tabB[:, bf, 0, :, :], invt[0, tc].rearrange("(g p) c -> p g c", p=128)),
                (tabB[:, bf, 1, :, :], invt[1, tc].rearrange("(g p) c -> p g c", p=128)),
            ], deps=[ev_tabB2[bf], ev_mmF if tc < 2 else None])
            def mmI(te, tc=tc, bf=bf):
                for kc in range(NKC):
                    nc.tensor.matmul(ps[0][:, :], tabB[:, bf, 0, kc, :],
                                     ufP[:, kc, 0, :], start=(kc == 0), stop=False)
                    last = nc.tensor.matmul(ps[0][:, :], tabB[:, bf, 1, kc, :],
                                            ufP[:, kc, 1, :], start=False, stop=(kc == NKC - 1))
                return last
            ev_mi = p.step("tensor", mmI, deps=[ev_tl, ev_pp, ev_inv_ps])
            ev_tabB2[bf] = ev_mi
            def vgel(v, tc=tc):
                return v.scalar_tensor_tensor(scr1[:, :], rT[:, tc, :], rsp[:, 1:2],
                                              ps[0][:, :], ALU.mult, ALU.add)
            ev_vg = p.step("vector", vgel, deps=[ev_mi, ev_sg_prev])
            ev_inv_ps = ev_vg
            def sgel(s, tc=tc):
                return s.activation(sc3[:, :], scr1[:, :], AF.Gelu)
            ev_sg = p.step("scalar", sgel, deps=[ev_vg, ev_ydma.get(0), ev_ydma.get(1)])
            ev_sg_prev = ev_sg
            for b in range(4):
                ysl = b % 2
                def tb_(te, tc=tc, b=b):
                    return nc.tensor.transpose(psT[:, 0:128], sc3[:, b * 128:(b + 1) * 128], idb[:])
                ev_tb = p.step("tensor", tb_, deps=[ev_sg, ev_yst.get("last")])
                def eyb(v, tc=tc, b=b, ysl=ysl):
                    return v.tensor_copy(yst[:, ysl, :], psT[:, 0:128])
                ev_ey = p.step("vector", eyb, deps=[ev_tb, ev_ydma.get(ysl)])
                ev_yst["last"] = ev_ey
                ev_ydma[ysl] = dma_step(p, "sync",
                                        [(yT_h[:, b * 4096 + tc * 128: b * 4096 + (tc + 1) * 128],
                                          yst[:, ysl, :])], deps=[ev_ey], exact=False)
        p.final_wait("sync")
        p.emit()
    return nc


_L1_TABLES = {}


def _l1_tables():
    if "kidft" in _L1_TABLES:
        return _L1_TABLES
    t = np.arange(SEQ, dtype=np.float64)
    l = np.arange(SEQ, dtype=np.float64)
    th1 = 2.0 * np.pi * np.outer(l, t) / SEQ
    kid = np.stack([np.cos(th1) / SEQ, -np.sin(th1) / SEQ]).astype(BF)
    _L1_TABLES["kidft"] = np.ascontiguousarray(kid.reshape(2, 32, 128, SEQ))
    k = np.arange(4224, dtype=np.float64)
    th2 = 2.0 * np.pi * np.outer(t, k) / 8192.0
    valid = (k <= 4096).astype(np.float64)
    fw = np.stack([np.cos(th2) * valid, -np.sin(th2) * valid]).astype(BF)
    _L1_TABLES["fwdt"] = np.ascontiguousarray(fw.reshape(2, SEQ, 33, 128).transpose(0, 2, 1, 3))
    ck = np.where((k == 0) | (k == 4096), 1.0, 2.0) * valid / 8192.0
    th3 = 2.0 * np.pi * np.outer(k, t) / 8192.0
    iv = np.stack([np.cos(th3) * ck[:, None], -np.sin(th3) * ck[:, None]]).astype(BF)
    _L1_TABLES["invt"] = np.ascontiguousarray(iv.reshape(2, 4224, 32, 128).transpose(0, 2, 1, 3))
    z = np.exp(-2j * np.pi * np.arange(SEQ) / SEQ)
    _L1_TABLES["g0im"] = np.ascontiguousarray(
        (2.0 * (1.0 - z) / (1.0 + z)).imag.astype(np.float32).reshape(32, 128).T)
    w2 = 2.0 / (1.0 + z)
    _L1_TABLES["w2re"] = np.ascontiguousarray(w2.real.astype(np.float32).reshape(32, 128).T)
    _L1_TABLES["w2im"] = np.ascontiguousarray(w2.imag.astype(np.float32).reshape(32, 128).T)
    return _L1_TABLES


ONES_ROW = np.ones((1, 128), np.float32)


def run_l1b(xnT, B_re, B_im, C_re, C_im, P_re, P_im, Q_re, Q_im, diag_re, diag_im, step, Dv):
    nc = _cached("l1b", build_l1b)
    f32 = np.float32
    tb = _l1_tables()
    rep = lambda v: np.ascontiguousarray(np.broadcast_to(np.asarray(v, f32), (128, 64)))
    qc = np.ascontiguousarray(np.stack([np.asarray(Q_re, f32), np.asarray(Q_im, f32),
                                        np.asarray(B_re, f32), np.asarray(B_im, f32),
                                        np.asarray(P_re, f32), np.asarray(P_im, f32)], 1))
    common = {
        "kidft": tb["kidft"], "fwdt": tb["fwdt"], "invt": tb["invt"],
        "g0im": tb["g0im"], "w2re": tb["w2re"], "w2im": tb["w2im"],
        "dre_b": rep(diag_re), "dim_b": rep(diag_im),
        "bre_b": rep(B_re), "bim_b": rep(B_im),
        "pre_b": rep(P_re), "pim_b": rep(P_im),
        "qcols": qc,
        "stepv": np.asarray(step, f32).reshape(1, 1),
        "dval": np.asarray(Dv, f32).reshape(1, 1),
        "onesf": ONES_ROW, "ident": IDENT128,
    }
    in_maps = []
    for c in range(NCORES):
        in_maps.append(dict(common,
                            rrows=np.ascontiguousarray(xnT[c * 128:(c + 1) * 128, :]),
                            cre=np.ascontiguousarray(np.asarray(C_re, f32)[c * 128:(c + 1) * 128, :]),
                            cim=np.ascontiguousarray(np.asarray(C_im, f32)[c * 128:(c + 1) * 128, :])))
    res = run_bass_kernel_spmd(nc, in_maps, core_ids=list(range(NCORES)))
    return np.concatenate([r["yT"] for r in res.results], axis=0)



def build_l1c():
    nc = bass.Bass("TRN2", target_bir_lowering=False, debug=False)
    x = nc.dram_tensor("x", [TPC, D], F32, kind="ExternalInput").ap()
    yT_h = nc.dram_tensor("yT", [D, TPC], BF16, kind="ExternalInput").ap()
    ident = nc.dram_tensor("ident", [128, 128], BF16, kind="ExternalInput").ap()
    y = nc.dram_tensor("y", [TPC, D], F32, kind="ExternalOutput").ap()
    with ExitStack() as es:
        sb = lambda name, shape, dt=F32: es.enter_context(nc.sbuf_tensor(name, shape, dt))
        xf = sb("xf", [128, NT, D])
        yts = sb("yts", [128, 8, TPC], BF16)
        idb = sb("idb", [128, 128], BF16)
        psT = es.enter_context(nc.psum_tensor("psT", [128, 1024], BF16))
        p = es.enter_context(Prog(nc))
        ev_x = [dma_step(p, "gpsimd", [(xf[:, t, :], x[t * 128:(t + 1) * 128, :])]) for t in range(NT)]
        ev_y = dma_step(p, "gpsimd", [(yts[:, :, :], yT_h.rearrange("(g p) c -> p g c", p=128)),
                                      (idb[:], ident)])
        ytm = es.enter_context(nc.sbuf_tensor("ytm", [128, 2, D], BF16))
        prev_ev = {}
        for t in range(NT):
            sl = t % 2
            xk = p.xkey(1)
            def ttr(e, t=t, sl=sl, xk=xk):
                for g in range(8):
                    e.dma_start(out=ytm[:, sl, g * 128:(g + 1) * 128],
                                in_=yts[:, g, t * 128:(t + 1) * 128],
                                transpose=True).then_inc(p.sem(xk), 16)
            ev_t = p.step("sync", ttr, deps=[ev_y, prev_ev.get(sl)], inc=16 * 8, self_inc=True, key=xk)
            def vadd(v, t=t, sl=sl):
                return v.tensor_tensor(xf[:, t, :], ytm[:, sl, :], xf[:, t, :], ALU.add)
            ev_v = p.step("vector", vadd, deps=[ev_t, ev_x[t]])
            prev_ev[sl] = ev_v
            dma_step(p, "sync", [(y[t * 128:(t + 1) * 128, :], xf[:, t, :])], deps=[ev_v], exact=False)
        p.final_wait("sync")
        p.emit()
    return nc


def run_l1c(x_full, yT):
    nc = _cached("l1c", build_l1c)
    shards = split_tokens(x_full)
    in_maps = [{"x": shards[c], "yT": np.ascontiguousarray(yT[:, c * TPC:(c + 1) * TPC]),
                "ident": IDENT128} for c in range(NCORES)]
    res = run_bass_kernel_spmd(nc, in_maps, core_ids=list(range(NCORES)))
    return join_tokens([r["y"] for r in res.results])


def run_l1attn(x_full, B_re, B_im, C_re, C_im, P_re, P_im, Q_re, Q_im, diag_re, diag_im, step, Dv):
    xnT = run_lnT(x_full)
    yT = run_l1b(xnT, B_re, B_im, C_re, C_im, P_re, P_im, Q_re, Q_im, diag_re, diag_im, step, Dv)
    return run_l1c(x_full, yT)



def kernel(**inputs):
    i = inputs
    x = np.ascontiguousarray(np.asarray(i["L0_attn_x"], np.float32))
    x = run_l0attn(x, i["L0_attn_norm_g"], i["L0_attn_norm_b"], i["L0_attn_Wq"],
                   i["L0_attn_Wkv"], i["L0_attn_q_scale"], i["L0_attn_k_scale"],
                   i["L0_attn_null_kv"], i["L0_attn_Wout"])
    x = run_l0ffn(x, i["L0_ffn_router_w"], i["L0_ffn_expert_weights"])
    x = run_l1attn(x, i["L1_attn_B_re"], i["L1_attn_B_im"], i["L1_attn_C_re"],
                   i["L1_attn_C_im"], i["L1_attn_P_re"], i["L1_attn_P_im"],
                   i["L1_attn_Q_re"], i["L1_attn_Q_im"], i["L1_attn_diag_re"],
                   i["L1_attn_diag_im"], i["L1_attn_step"], i["L1_attn_D"])
    x = run_l1ffn(x, i["L1_ffn_Win"], i["L1_ffn_bin_"], i["L1_ffn_W1"], i["L1_ffn_b1"],
                  i["L1_ffn_W2"], i["L1_ffn_b2"], i["L1_ffn_Wout"], i["L1_ffn_bout"])
    x = run_l2attn(x, i["L2_attn_W_in"], i["L2_attn_conv_w"], i["L2_attn_conv_b"],
                   i["L2_attn_A_log"], i["L2_attn_D"], i["L2_attn_W_out"])
    x = run_l2ffn(x, i["L2_ffn_router_w"], i["L2_ffn_w1"], i["L2_ffn_w2"])
    x = run_l3attn(x, i["L3_attn_Wq"], i["L3_attn_bq"], i["L3_attn_Wk"], i["L3_attn_bk"],
                   i["L3_attn_Wv"], i["L3_attn_bv"], i["L3_attn_Wo"], i["L3_attn_bo"])
    x = run_l3ffn(x, i["L3_ffn_weight"], i["L3_ffn_bias"])
    return x.astype(np.float32)
